# version 15 (final)
# baseline (speedup 1.0000x reference)
import contextlib
import os
import numpy as np
import ml_dtypes
import concourse.bass as bass
import concourse.mybir as mybir
from concourse.bass_utils import run_bass_kernel_spmd

F32 = mybir.dt.float32
BF16 = mybir.dt.bfloat16
BF = ml_dtypes.bfloat16
AF = mybir.ActivationFunctionType
ALU = mybir.AluOpType
AX = mybir.AxisListType.X

B, T, C, H, HD = 2, 8192, 1024, 16, 64
NT, QT = 16, 128
TOK = NT * QT
NHP = H // 2
NE, I_FF = 16, 512
EXP_BIAS = -12.0
EPS = 1e-5
STAGE = 6

ENGS = ("sync", "scalar", "vector", "tensor", "gpsimd")


class Sem:
    def __init__(self, nc, es, name):
        self.h = es.enter_context(nc.semaphore(name))
        self.n = 0
        self.name = name

    def bump(self, k):
        self.n += k
        return self.n


class Prog:
    def __init__(self, nc, es):
        self.nc = nc
        self.es = es
        self.stream = {e: [] for e in ENGS}
        self.hw = {e: {} for e in ENGS}

    def sem(self, name):
        return Sem(self.nc, self.es, name)

    def emit(self, eng, builder, waits=(), incs=()):
        waits = tuple(waits)
        for s, k in incs:
            if k == 16:
                waits = ((s, s.n),) + waits
        wl = []
        for s, v in waits:
            if self.hw[eng].get(s.name, -1) < v:
                self.hw[eng][s.name] = v
                wl.append((s.h, v))
        il = []
        for s, k in incs:
            s.bump(k)
            il.append((s.h, k, s.n))
        self.stream[eng].append((builder, wl, il))

    def wait(self, eng, sem, val):
        self.emit(eng, None, waits=((sem, val),))

    def finalize(self, block):
        def mk(eng_name):
            def run(engine):
                for builder, wl, il in self.stream[eng_name]:
                    for h, v in wl:
                        engine.wait_ge(h, v)
                    if builder is None:
                        continue
                    inst = builder(engine)
                    for h, k, _ in il:
                        inst.then_inc(h, k)
            return run
        block.sync(mk("sync"))
        block.scalar(mk("scalar"))
        block.vector(mk("vector"))
        block.tensor(mk("tensor"))
        block.gpsimd(mk("gpsimd"))


def build_nc():
    nc = bass.Bass("TRN2", target_bir_lowering=False, detect_race_conditions=False)
    es = contextlib.ExitStack()
    p = Prog(nc, es)

    def din(name, shape, dt=F32):
        return nc.dram_tensor(name, shape, dt, kind="ExternalInput").ap()

    x_own = din("x_own", [TOK, C])
    x_kv = din("x_kv", [T, C])
    masks = din("masks", [4, 128, 128], BF16)
    identf = din("identf", [128, 128], F32)
    W_attn = din("W_attn", [3 * C, C])
    b_attn = din("b_attn", [3 * C])
    W_proj = din("W_proj", [C, C])
    b_proj = din("b_proj", [C])
    gate_w = din("gate_w", [C, NE])
    gate_b = din("gate_b", [NE])
    sh_wg = din("sh_wg", [C, I_FF])
    sh_wu = din("sh_wu", [C, I_FF])
    sh_wd = din("sh_wd", [I_FF, C])
    ex_wg = din("ex_wg", [NE, C, I_FF])
    ex_wu = din("ex_wu", [NE, C, I_FF])
    ex_wd = din("ex_wd", [NE, I_FF, C])
    selb = din("selb", [16, 2048], BF16)

    out_c = nc.dram_tensor("out_c", [TOK, C], F32, kind="ExternalOutput").ap()
    dbg = {}
    if STAGE == 1:
        dbg["qT"] = nc.dram_tensor("dbg_qT", [C, TOK], BF16, kind="ExternalOutput").ap()
        dbg["xlnT"] = nc.dram_tensor("dbg_xlnT", [C, TOK], BF16, kind="ExternalOutput").ap()
        dbg["st"] = nc.dram_tensor("dbg_st", [128, 64], F32, kind="ExternalOutput").ap()
    if STAGE == 2:
        dbg["kslab"] = nc.dram_tensor("dbg_kslab", [NHP, 128, T], BF16, kind="ExternalOutput").ap()
        dbg["vslab"] = nc.dram_tensor("dbg_vslab", [NHP, T, 130], BF16, kind="ExternalOutput").ap()
    if STAGE == 3:
        dbg["yT"] = nc.dram_tensor("dbg_yT", [C, TOK], BF16, kind="ExternalOutput").ap()
    if STAGE == 4:
        dbg["x2"] = nc.dram_tensor("dbg_x2", [TOK, C], F32, kind="ExternalOutput").ap()
        dbg["xln2T"] = nc.dram_tensor("dbg_xln2T", [C, TOK], BF16, kind="ExternalOutput").ap()
    if False:
        dbg["acc0"] = nc.dram_tensor("dbg_acc0", [128, 2048], F32, kind="ExternalOutput").ap()
        dbg["acc1"] = nc.dram_tensor("dbg_acc1", [128, 2048], F32, kind="ExternalOutput").ap()
        dbg["ffw"] = nc.dram_tensor("dbg_ffw", [128, 4096], BF16, kind="ExternalOutput").ap()
        dbg["hp0"] = nc.dram_tensor("dbg_hp0", [128, 1024], BF16, kind="ExternalOutput").ap()
        dbg["oev"] = nc.dram_tensor("dbg_oev", [128, 2048], F32, kind="ExternalOutput").ap()
    if STAGE == 5:
        dbg["wT"] = nc.dram_tensor("dbg_wT", [NE, TOK], BF16, kind="ExternalOutput").ap()
        dbg["rT"] = nc.dram_tensor("dbg_rT", [NE, TOK], F32, kind="ExternalOutput").ap()
        dbg["rtok"] = nc.dram_tensor("dbg_rtok", [128, 64], F32, kind="ExternalOutput").ap()

    kslab = nc.dram_tensor("kslab", [NHP, 128, T], BF16, kind="Internal").ap()
    vslab = nc.dram_tensor("vslab", [NHP, T, 130], BF16, kind="Internal").ap()
    x2sp = nc.dram_tensor("x2sp", [TOK, C], F32, kind="Internal").ap()
    wslab = nc.dram_tensor("wslab", [17, 3, 128, 4096], BF16, kind="Internal").ap()
    ffacc = nc.dram_tensor("ffacc", [C, TOK], F32, kind="Internal").ap()

    def sb(name, shape, dt):
        return es.enter_context(nc.sbuf_tensor(name, shape, dt))

    AW = sb("AW", [128, 24576], BF16)
    AQ = sb("AQ", [128, 16384], BF16)
    AY = sb("AY", [128, 16384], BF16)
    AK = sb("AK", [128, 16512], BF16)
    xstg = sb("xstg", [128, 2048], F32)
    lnstg = sb("lnstg", [128, 2048], BF16)
    xstg2 = sb("xstg2", [128, 2048], F32)
    lnstg2 = sb("lnstg2", [128, 2048], BF16)
    identbf = sb("identbf", [128, 128], BF16)
    stats = sb("stats", [128, 16], F32)
    xlk = sb("xlk", [128, 4096], BF16)
    kvev = sb("kvev", [128, 1024], BF16)
    eT = sb("eT", [128, 2048], BF16)
    vst = sb("vst", [128, 2080], BF16)
    consts = sb("consts", [128, 136], F32)
    cbf = sb("cbf", [128, 512], BF16)
    rowbf = sb("rowbf", [1, 1152], BF16)
    rowf = sb("rowf", [1, 128], F32)
    small = sb("small", [128, 128], F32)
    hprime = sb("hprime", [128, 1024], BF16)
    t1s = sb("t1s", [128, 512], F32)
    oev = sb("oev", [128, 2048], F32)
    rtok = sb("rtok", [128, 64], F32)
    wsl = sb("wsl", [128, 64], F32)
    r2sl = sb("r2sl", [128, 64], F32)
    wstg = oev
    sqscr = eT[:, 0:1024]
    ffw = nc.virtual_sbuf_tensor("ffw", [128, 4096], BF16)
    nc.back_virtual_tensor_with(ffw, xlk)
    wT16 = vst[0:16, 0:2048]
    ofin = xstg2
    rT16 = oev[0:16, :]

    battn_sb = sb("battn", [128, 24], F32)
    bproj_sb = sb("bproj", [128, 8], F32)
    gateb_sb = sb("gateb", [16, 1], F32)

    identsb = consts[:, 0:128]
    expbias = consts[:, 128:129]
    lnepsb = consts[:, 129:130]
    halflnb = consts[:, 130:131]

    psD = [es.enter_context(nc.psum_tensor(f"psD{i}", [128, 1024], F32)) for i in range(4)]
    ps = [psD[i // 2][:, (i % 2) * 512:(i % 2) * 512 + 512] for i in range(8)]
    psg = {}

    def bank_wait(i):
        g = psg.get(i)
        return (g,) if g else ()

    def bank_consumed(i, sem, val):
        psg[i] = (sem, val)

    def vfence():
        E("vector", lambda e: e.tensor_copy(stats[:, 7:8], consts[:, 128:129]),
          incs=((sVE, 1),))
        return sVE.n

    def sfence():
        E("scalar", lambda e: e.copy(stats[:, 7:8], consts[:, 128:129]),
          incs=((sSC, 1),))
        return sSC.n

    dW = p.sem("dW")
    dX = p.sem("dX")
    dKV = p.sem("dKV")
    sPE = p.sem("sPE")
    sSC = p.sem("sSC")
    sVE = p.sem("sVE")
    sGP = p.sem("sGP")

    E = p.emit

    identbf_ready = [0]
    h2dma = [0]

    def P0():
        E("sync", lambda e, d=identsb, s=identf: e.dma_start(out=d, in_=s), incs=((dKV, 16),))
        E("sync", lambda e, d=cbf[:, 0:512].rearrange("p (r q) -> p r q", r=4),
          s=masks.rearrange("r kk qq -> kk r qq"): e.dma_start(out=d, in_=s), incs=((dKV, 16),))
        E("sync", lambda e, d=battn_sb[:], s=b_attn.rearrange("(t p) -> p t", p=128):
          e.dma_start(out=d, in_=s, allow_slow_non_contiguous=True), incs=((dKV, 16),))
        E("sync", lambda e, d=bproj_sb[:], s=b_proj.rearrange("(t p) -> p t", p=128):
          e.dma_start(out=d, in_=s, allow_slow_non_contiguous=True), incs=((dKV, 16),))
        E("sync", lambda e, d=gateb_sb[:], s=gate_b.rearrange("(a b) -> a b", b=1):
          e.dma_start(out=d, in_=s, allow_slow_non_contiguous=True), incs=((dKV, 16),))
        E("gpsimd", lambda e, d=rowbf[0:1, 0:1024], s=b_attn[2048:3072].rearrange("(a b) -> a b", a=1):
          e.dma_start(out=d, in_=s), incs=((dW, 16),))
        E("vector", lambda e: e.memset(expbias, EXP_BIAS), incs=((sVE, 1),))
        E("vector", lambda e: e.memset(lnepsb, EPS * 1024.0))
        E("vector", lambda e: e.memset(halflnb, 0.5 * float(np.log(1024.0))))
        E("vector", lambda e: e.memset(rowbf[0:1, 1024:1152], 1.0))
        E("vector", lambda e: e.memset(rowf[0:1, 0:128], 1.0))
        E("vector", lambda e: e.memset(vst[:], 0.0))
        for bb in range(2):
            base = 1040 * bb
            E("vector", lambda e, b2=base: e.memset(
                vst[:, b2:b2 + 1040].rearrange("p (hp c) -> p hp c", hp=8)[:, :, 64:65], 1.0),
              incs=((sVE, 1),))
            E("vector", lambda e, b2=base: e.memset(
                vst[:, b2:b2 + 1040].rearrange("p (hp c) -> p hp c", hp=8)[:, :, 129:130], 1.0),
              incs=((sVE, 1),))
        E("vector", lambda e: e.memset(small[:, 0:64], 1.0), incs=((sVE, 1),))
        E("vector", lambda e: e.tensor_copy(identbf[:], identsb), waits=((dKV, 16),))
        identbf_ready[0] = vfence()

    cast_dw = {}

    def P0c():
        pace = ((dKV, dKV.n + (288 if STAGE >= 3 else 0)),)
        for e_ in range(17):
            if e_ == 0:
                srcs = (sh_wg, sh_wu, sh_wd)
            else:
                srcs = (ex_wg[e_ - 1], ex_wu[e_ - 1], ex_wd[e_ - 1])
            E("gpsimd", lambda e, d=wslab[e_, 0], s=srcs[0].rearrange("(ct p) i -> p ct i", p=128):
              e.dma_start(out=d.rearrange("p (ct i) -> p ct i", ct=8), in_=s),
              waits=(pace if e_ == 0 else ()), incs=((dW, 16),))
            E("gpsimd", lambda e, d=wslab[e_, 1], s=srcs[1].rearrange("(ct p) i -> p ct i", p=128):
              e.dma_start(out=d.rearrange("p (ct i) -> p ct i", ct=8), in_=s), incs=((dW, 16),))
            E("gpsimd", lambda e, d=wslab[e_, 2], s=srcs[2].rearrange("(ct p) d2 -> p ct d2", p=128):
              e.dma_start(out=d.rearrange("p (ct d2) -> p ct d2", ct=4), in_=s), incs=((dW, 16),))
            cast_dw[e_] = dW.n

    def WA(ct):
        return AW[:, ct * 3072:(ct + 1) * 3072]

    def P0_wattn():
        for ft in range(24):
            buf = wstg[:, (ft % 2) * 1024:(ft % 2) * 1024 + 1024]
            if ft >= 2:
                E("sync", None, waits=((sPE, p_done[ft - 2]),))
            E("sync", lambda e, d=buf, s=W_attn[ft * 128:(ft + 1) * 128, :]:
              e.dma_start(out=d, in_=s), incs=((dKV, 16),))
            v = dKV.n
            for ct in range(8):
                bk = ct % 2
                bank = ps[bk]
                E("tensor", lambda e, o=bank[:, 0:128], i=buf[:, ct * 128:(ct + 1) * 128]:
                  e.transpose(o, i, identsb),
                  waits=((dKV, v),) + bank_wait(bk), incs=((sPE, 1),))
                pe_v = sPE.n
                E("vector", lambda e, d=WA(ct)[:, ft * 128:(ft + 1) * 128], s=bank[:, 0:128]:
                  e.tensor_copy(d, s), waits=((sPE, pe_v),), incs=((sVE, 1),))
                bank_consumed(bk, sVE, sVE.n)
            p_done[ft] = sPE.n

    p_done = {}
    ev_done = {}

    ln_prev = {}
    def emit_ln_pair(xA, oA, xB, oB, n, wait_x=(), war_wait=()):
        stA = stats[:, 0:7]
        stB = stats[:, 8:15]
        wx = tuple(wait_x) + tuple(war_wait) + ((sSC, ln_prev.get(0, 0)),)
        E("scalar", lambda e, o=sqscr[:, 0:n], i=xA, acc=stA[:, 0:1]:
          e.activation(o, i, AF.Identity, bias=0.0, scale=1.0 / n, accum_out=acc),
          waits=wx)
        E("scalar", lambda e, o=sqscr[:, 0:n], i=xB, acc=stB[:, 0:1]:
          e.activation(o, i, AF.Identity, bias=0.0, scale=1.0 / n, accum_out=acc))
        E("scalar", lambda e, o=stA[:, 1:2], i=stA[:, 0:1]: e.mul(o, i, -1.0))
        E("scalar", lambda e, o=stB[:, 1:2], i=stB[:, 0:1]: e.mul(o, i, -1.0))
        E("scalar", lambda e, o=sqscr[:, 0:n], i=xA, bi=stA[:, 1:2], acc=stA[:, 2:3]:
          e.activation(o, i, AF.Square, bias=bi, scale=1.0, accum_out=acc))
        E("scalar", lambda e, o=sqscr[:, 0:n], i=xB, bi=stB[:, 1:2], acc=stB[:, 2:3]:
          e.activation(o, i, AF.Square, bias=bi, scale=1.0, accum_out=acc))
        E("scalar", lambda e, o=stA[:, 3:4], i=stA[:, 2:3]:
          e.activation(o, i, AF.Ln, bias=lnepsb, scale=1.0))
        E("scalar", lambda e, o=stB[:, 3:4], i=stB[:, 2:3]:
          e.activation(o, i, AF.Ln, bias=lnepsb, scale=1.0))
        E("scalar", lambda e, o=stA[:, 4:5], i=stA[:, 3:4]:
          e.activation(o, i, AF.Exp, bias=halflnb, scale=-0.5))
        E("scalar", lambda e, o=stB[:, 4:5], i=stB[:, 3:4]:
          e.activation(o, i, AF.Exp, bias=halflnb, scale=-0.5))
        E("scalar", lambda e, o=stA[:, 5:6], i=stA[:, 1:2], mm=stA[:, 4:5]: e.mul(o, i, mm))
        E("scalar", lambda e, o=stB[:, 5:6], i=stB[:, 1:2], mm=stB[:, 4:5]: e.mul(o, i, mm))
        E("scalar", lambda e, o=oA, i=xA, sc=stA[:, 4:5], bi=stA[:, 5:6]:
          e.activation(o, i, AF.Identity, bias=bi, scale=sc), incs=((sSC, 1),))
        E("scalar", lambda e, o=oB, i=xB, sc=stB[:, 4:5], bi=stB[:, 5:6]:
          e.activation(o, i, AF.Identity, bias=bi, scale=sc), incs=((sSC, 1),))
        ln_prev[0] = sSC.n
        return sSC.n

    def emit_transpose_cast(srctile, dest_fn, wait_sc, bankbase=6, extra_waits=()):
        for cp in range(4):
            bk = bankbase + (cp % 2)
            bank = ps[bk]
            for k in range(2):
                ct = 2 * cp + k
                E("tensor", lambda e, o=bank[:, k * 128:k * 128 + 128],
                  i=srctile[:, ct * 128:(ct + 1) * 128]:
                  e.matmul(o, i, identbf[:], start=True, stop=True),
                  waits=(((sSC, wait_sc), (sVE, identbf_ready[0])) + bank_wait(bk))
                        if k == 0 else (),
                  incs=(((sPE, 1),) if k == 1 else ()))
            pv = sPE.n
            E("vector", lambda e, d=dest_fn(cp),
              s=bank[:, 0:256].rearrange("p (c x) -> p c x", c=2):
              e.tensor_copy(d, s),
              waits=((sPE, pv),) + (tuple(extra_waits) if cp == 0 else ()),
              incs=((sVE, 1),))
            bank_consumed(bk, sVE, sVE.n)
        return sVE.n

    def XLN(ct):
        return AY[:, ct * 2048:(ct + 1) * 2048]

    def QT_(hp):
        return AQ[:, hp * 2048:(hp + 1) * 2048]

    def XLNp(cp, t2):
        return AY[:, 0:16384].rearrange("p (c x) -> p c x", c=8)[
            :, 2 * cp:2 * cp + 2, t2 * 128:(t2 + 1) * 128]

    def XLKp(cp, j):
        return xlk[:, 0:4096].rearrange("p (c x) -> p c x", c=8)[
            :, 2 * cp:2 * cp + 2, j * 128:(j + 1) * 128]

    def XL2p(cp, t2):
        return AQ[:, 0:16384].rearrange("p (c x) -> p c x", c=8)[
            :, 2 * cp:2 * cp + 2, t2 * 128:(t2 + 1) * 128]

    def P1():
        def emit_pair(tp):
            tA, tB = 2 * tp, 2 * tp + 1
            xb = xstg if tp % 2 == 0 else xstg2
            lb = lnstg if tp % 2 == 0 else lnstg2
            xbA = xb[:, 0:1024]
            xbB = xb[:, 1024:2048]
            lbA = lb[:, 0:1024]
            lbB = lb[:, 1024:2048]
            w = ((sVE, tr_done[tp - 2]),) if tp >= 2 else ()
            E("scalar", lambda e, d=xbA, s=x_own[tA * 128:(tA + 1) * 128, :]:
              e.dma_start(out=d, in_=s), waits=w, incs=((dX, 16),))
            E("scalar", lambda e, d=xbB, s=x_own[tB * 128:(tB + 1) * 128, :]:
              e.dma_start(out=d, in_=s), incs=((dX, 16),))
            vx = dX.n
            sc = emit_ln_pair(xbA, lbA, xbB, lbB, C, wait_x=((dX, vx),), war_wait=w)
            emit_transpose_cast(lbA, lambda cp, t2=tA: XLNp(cp, t2), sc)
            tr_done[tp] = emit_transpose_cast(
                lbB, lambda cp, t2=tB: XLNp(cp, t2), sc)

        def emit_qproj(ch):
            wlast = tr_done[2 * ch + 1]
            for hp in range(8):
                bk = (hp * 4 + ch) % 4
                bank = ps[bk]
                for ct in range(8):
                    E("tensor", lambda e, o=bank[:], l=WA(ct)[:, hp * 128:(hp + 1) * 128],
                      r=XLN(ct)[:, ch * 512:(ch + 1) * 512], st=(ct == 0), sp=(ct == 7):
                      e.matmul(o, l, r, start=st, stop=sp),
                      waits=(((sVE, wlast),) + bank_wait(bk)) if ct == 0 else (),
                      incs=(((sPE, 1),) if ct == 7 else ()))
                pv = sPE.n
                E("vector", lambda e, d=QT_(hp)[:, ch * 512:(ch + 1) * 512], s=bank[:],
                  bi=battn_sb[:, hp:hp + 1]:
                  e.tensor_scalar(d, s, bi, None, ALU.add),
                  waits=((sPE, pv),), incs=((sVE, 1),))
                bank_consumed(bk, sVE, sVE.n)
                qevac_done[hp * 4 + ch] = sVE.n

        for tp in range(NT // 2):
            emit_pair(tp)
            if tp >= 3 and tp % 2 == 1:
                emit_qproj((tp - 3) // 2)
        emit_qproj(3)

    tr_done = {}
    qevac_done = {}
    p2_mm_done = {}
    p2_pg0 = NT // 2
    p4_pg0 = NT // 2 + 32
    wp_done = {}
    spill_done = {}
    out_stg_free = {}

    def P1_dbg():
        if STAGE != 1:
            return
        E("scalar", None, waits=((sVE, qevac_done[31]),))
        for ct in range(8):
            E("scalar", lambda e, d=dbg["xlnT"][ct * 128:(ct + 1) * 128, :], s=XLN(ct):
              e.dma_start(out=d, in_=s), incs=((dX, 16),))
        for hp in range(8):
            E("scalar", lambda e, d=dbg["qT"][hp * 128:(hp + 1) * 128, :], s=QT_(hp):
              e.dma_start(out=d, in_=s), incs=((dX, 16),))
        E("scalar", lambda e, d=dbg["st"], s=small[:, 0:64]: e.dma_start(out=d, in_=s),
          incs=((dX, 16),))
        E("scalar", None, waits=((dX, dX.n),))
        E("scalar", lambda e, d=out_c[0:128, :], s=xstg[:, 0:1024]:
          e.dma_start(out=d, in_=s), incs=((dX, 16),))
        E("scalar", None, waits=((dX, dX.n),))

    def XLK(ct):
        return xlk[:, ct * 512:(ct + 1) * 512]

    def P2():
        kvev_free = {}
        vst_free = {}
        ln_sc = {}

        def emit_ln(cc):
            for pr in range(2):
                tA = 4 * cc + 2 * pr
                tB = tA + 1
                pg = p2_pg0 + 2 * cc + pr
                xb = xstg if pg % 2 == 0 else xstg2
                lb = lnstg if pg % 2 == 0 else lnstg2
                xbA = xb[:, 0:1024]
                xbB = xb[:, 1024:2048]
                lbA = lb[:, 0:1024]
                lbB = lb[:, 1024:2048]
                w = ((sVE, tr_done[pg - 2]),) if (pg - 2) in tr_done else ()
                E("scalar", lambda e, d=xbA, s=x_kv[tA * 128:(tA + 1) * 128, :]:
                  e.dma_start(out=d, in_=s), waits=w, incs=((dX, 16),))
                E("scalar", lambda e, d=xbB, s=x_kv[tB * 128:(tB + 1) * 128, :]:
                  e.dma_start(out=d, in_=s), incs=((dX, 16),))
                vx = dX.n
                ln_sc[(cc, pr)] = emit_ln_pair(xbA, lbA, xbB, lbB, C,
                                               wait_x=((dX, vx),), war_wait=w)

        def emit_tile_tr(cc, j):
            pr = j // 2
            pg = p2_pg0 + 2 * cc + pr
            lb = lnstg if pg % 2 == 0 else lnstg2
            lbt = lb[:, (j % 2) * 1024:(j % 2) * 1024 + 1024]
            v = emit_transpose_cast(
                lbt, lambda cp, j2=j: XLKp(cp, j2), ln_sc[(cc, pr)])
            if j % 2 == 1:
                tr_done[pg] = v

        def emit_K(cc):
            wx = tr_done[p2_pg0 + 2 * cc + 1]
            for kf in range(8):
                bk = kf % 4
                bank = ps[bk]
                for ct in range(8):
                    E("tensor", lambda e, o=bank[:], l=WA(ct)[:, 1024 + kf * 128:1024 + (kf + 1) * 128],
                      r=XLK(ct), st=(ct == 0), sp=(ct == 7):
                      e.matmul(o, l, r, start=st, stop=sp),
                      waits=(((sVE, wx),) + bank_wait(bk)) if ct == 0 else (),
                      incs=(((sPE, 1),) if ct == 7 else ()))
                pv = sPE.n
                kb = kf % 2
                kbuf = kvev[:, kb * 512:(kb + 1) * 512]
                E("vector", lambda e, d=kbuf, s=bank[:], bi=battn_sb[:, 8 + kf:9 + kf]:
                  e.tensor_scalar(d, s, bi, None, ALU.add),
                  waits=((sPE, pv), (dKV, kvev_free.get(kb, 0))), incs=((sVE, 1),))
                bank_consumed(bk, sVE, sVE.n)
                scv = sVE.n
                E("sync", lambda e, d=kslab[kf, :, cc * 512:(cc + 1) * 512], s=kbuf:
                  e.dma_start(out=d, in_=s), waits=((sVE, scv),), incs=((dKV, 16),))
                kvev_free[kb] = dKV.n

        def emit_V(cc, tj):
            bk = 4 + (tj % 2)
            bank = ps[bk]
            vb = tj % 2
            vbuf = vst[:, vb * 1040:(vb + 1) * 1040]
            for vc in range(2):
                for ct in range(8):
                    E("tensor", lambda e, o=bank[:], l=XLK(ct)[:, tj * 128:(tj + 1) * 128],
                      r=WA(ct)[:, 2048 + vc * 512:2048 + (vc + 1) * 512],
                      st=(ct == 0), sp=False:
                      e.matmul(o, l, r, start=st, stop=sp),
                      waits=(bank_wait(bk) if ct == 0 else ()),
                      incs=())
                E("tensor", lambda e, o=bank[:], l=rowbf[0:1, 1024:1152],
                  r=rowbf[0:1, vc * 512:(vc + 1) * 512]:
                  e.matmul(o, l, r, start=False, stop=True),
                  incs=((sPE, 1),))
                pv = sPE.n
                pv_s = bank[:].rearrange("p (hp hh d) -> p hp hh d", hp=4, hh=2)
                E("vector", lambda e, s=pv_s[:, :, 0, :],
                  d=vbuf.rearrange("p (hp y) -> p hp y", hp=8)[:, 4 * vc:4 * vc + 4, 0:64]:
                  e.tensor_copy(d, s),
                  waits=((sPE, pv), (dKV, vst_free.get(vb, 0))))
                E("vector", lambda e, s=pv_s[:, :, 1, :],
                  d=vbuf.rearrange("p (hp y) -> p hp y", hp=8)[:, 4 * vc:4 * vc + 4, 65:129]:
                  e.tensor_copy(d, s), incs=((sVE, 1),))
                bank_consumed(bk, sVE, sVE.n)
            vv = sVE.n
            E("sync", lambda e,
              d=vslab[:, cc * 512 + tj * 128: cc * 512 + (tj + 1) * 128, :]
                .rearrange("hp r y -> r hp y"),
              s=vbuf.rearrange("p (hp y) -> p hp y", hp=8):
              e.dma_start(out=d, in_=s), waits=((sVE, vv),), incs=((dKV, 16),))
            vst_free[vb] = dKV.n

        for cc in range(16):
            emit_ln(cc)
            if cc == 0:
                for j in range(4):
                    emit_tile_tr(0, j)
                continue
            emit_K(cc - 1)
            for tj in range(4):
                emit_V(cc - 1, tj)
                emit_tile_tr(cc, tj)
        emit_K(15)
        for tj in range(4):
            emit_V(15, tj)
        p2_mm_done[15] = sPE.n

    def P2_dbg():
        if STAGE != 2:
            return
        E("sync", None, waits=((dKV, dKV.n),))
        E("sync", lambda e, d=dbg["kslab"], s=kslab: e.dma_start(out=d, in_=s),
          incs=((dKV, 16),))
        E("sync", lambda e, d=dbg["vslab"], s=vslab: e.dma_start(out=d, in_=s),
          incs=((dKV, 16),))
        E("sync", None, waits=((dKV, dKV.n),))
        E("scalar", lambda e, d=out_c[0:128, :], s=xstg[:, 0:1024]:
          e.dma_start(out=d, in_=s), incs=((dX, 16),))
        E("scalar", None, waits=((dX, dX.n),))


    def P3():
        kvbuf = [(AK[:, 0:8192], AK[:, 8192:16512]),
                 (AW[:, 0:8192], AW[:, 8192:16512])]
        kv_ready = {}
        hp_mm_done = {}
        p2_all = dKV.n
        ybank = {(0, 0): 4, (0, 1): 5, (1, 0): 6, (1, 1): 7}
        pending = []
        norm_queue = []
        rw_set = [0, 0, 0, 0]
        estep = [0]

        def emit_av(avd):
            (hp2, par2, pvexp, mvs, avlist, kvb2) = avd
            for (h, yb, co, w2, tile, st_flag, sp_flag) in avlist:
                wts2 = [(sSC, pvexp)]
                if mvs is not None:
                    wts2.append((sVE, mvs))
                if st_flag:
                    wts2 += list(bank_wait(yb))
                E("tensor", lambda e, o=ps[yb][0:65, co:co + w2],
                  l=kvb2[:, tile * 130 + 65 * h: tile * 130 + 65 * h + 65],
                  r2=eT[:, par2 * 1024 + 512 * h: par2 * 1024 + 512 * h + w2],
                  st=st_flag, sp=sp_flag:
                  e.matmul(o, l, r2, start=st, stop=sp, skip_group_check=True),
                  waits=tuple(wts2), incs=((sPE, 1),))
                if sp_flag:
                    qc2 = {4: 0, 5: 1, 6: 0, 7: 1}[yb]
                    si = 2 * h + qc2
                    pvs2 = sPE.n
                    E("scalar", lambda e, o=oev[64:65, si * 512:(si + 1) * 512],
                      i=ps[yb][64:65, 0:512]:
                      e.activation(o, i, AF.Ln, bias=0.0, scale=1.0),
                      waits=((sPE, pvs2),))
                    E("scalar", lambda e, o=oev[64:65, si * 512:(si + 1) * 512],
                      i=oev[64:65, si * 512:(si + 1) * 512]: e.mul(o, i, -1.0))
                    E("scalar", lambda e, o=ofin[64:65, si * 512:(si + 1) * 512],
                      i=oev[64:65, si * 512:(si + 1) * 512]:
                      e.activation(o, i, AF.Exp, bias=0.0, scale=1.0), incs=((sSC, 1),))
                    rw_set[si] = sSC.n

        def flush_norm():
            while norm_queue:
                norm_queue.pop(0)()

        for hp in range(NHP):
            par = hp % 2
            kbuf, vbuf = kvbuf[par]
            bw = ((sPE, hp_mm_done.get(hp - 2, 0)),)
            E("sync", lambda e, d=kbuf, s=kslab[hp]: e.dma_start(out=d, in_=s),
              waits=((dKV, p2_all),) + bw, incs=((dKV, 16),))
            for tg in range(8):
                E("sync", lambda e,
                  d=vbuf.rearrange("p (tl y) -> p tl y", tl=64)[:, tg * 8:(tg + 1) * 8, :],
                  s=vslab[hp, tg * 1024:(tg + 1) * 1024, :].rearrange("(tl p) y -> p tl y", p=128):
                  e.dma_start(out=d, in_=s), incs=((dKV, 16),))
            kv_ready[hp] = dKV.n
            for qg in range(2):
                sched = []
                for a in range(8 * qg + 8):
                    su0 = max(qg * 1024, 128 * a)
                    su1 = qg * 1024 + 1024
                    subs = []
                    s0 = su0
                    nxt = 512 * (s0 // 512) + 512
                    while s0 < su1:
                        s1 = min(su1, nxt)
                        subs.append((s0, s1))
                        s0, nxt = s1, s1 + 512
                    for r in range(4):
                        for (s0b, s1b) in subs:
                            sched.append((a, r, s0b, s1b))
                last_touch = {}
                for idx, (a, r, s0b, s1b) in enumerate(sched):
                    qc = (s0b - qg * 1024) // 512
                    for h in range(2):
                        last_touch[(h, qc)] = idx
                started = set()
                for idx, (a, r, s0b, s1b) in enumerate(sched):
                    w = s1b - s0b
                    qc = (s0b - qg * 1024) // 512
                    co = s0b % 512
                    boundary = (a >= 8 * qg) and (s0b == max(qg * 1024, 128 * a))
                    pp = estep[0] % 2
                    estep[0] += 1
                    sD = psD[pp]
                    for h in range(2):
                        bkid = 2 * pp + h
                        E("tensor", lambda e, o=sD[:, 512 * h:512 * h + w],
                          l=kbuf[64 * h:64 * h + 64, a * 512 + r * 128:a * 512 + r * 128 + 128],
                          r2=QT_(hp)[64 * h:64 * h + 64, s0b:s1b], tp=(64 * h, 0):
                          e.matmul(o, l, r2, start=True, stop=True, tile_position=tp),
                          waits=((dKV, kv_ready[hp]),) + bank_wait(bkid),
                          incs=(((sPE, 1),) if h == 1 else ()))
                    pvs = sPE.n
                    eslot = eT[:, pp * 1024: pp * 1024 + 1024]
                    ein = sD[:].rearrange("p (h2 q) -> p h2 q", h2=2)[:, :, 0:w]
                    eout = eslot.rearrange("p (h2 q) -> p h2 q", h2=2)[:, :, 0:w]
                    E("scalar", lambda e, o=eout, i=ein:
                      e.activation(o, i, AF.Exp, bias=expbias, scale=0.125),
                      waits=((sPE, pvs),), incs=((sSC, 1),))
                    bank_consumed(2 * pp, sSC, sSC.n)
                    bank_consumed(2 * pp + 1, sSC, sSC.n)
                    pvexp = sSC.n
                    mvs = None
                    if boundary:
                        E("vector", lambda e, o=eslot[:, 0:128], m=cbf[:, r * 128:(r + 1) * 128]:
                          e.tensor_mul(o, o, m), waits=((sSC, pvexp),))
                        E("vector", lambda e, o=eslot[:, 512:640], m=cbf[:, r * 128:(r + 1) * 128]:
                          e.tensor_mul(o, o, m), incs=((sVE, 1),))
                        mvs = sVE.n
                    if idx == 1:
                        flush_norm()
                    if pending:
                        emit_av(pending.pop(0))
                    avlist = []
                    for h in range(2):
                        yb = ybank[(h, qc)]
                        st_flag = (h, qc) not in started
                        started.add((h, qc))
                        sp_flag = last_touch[(h, qc)] == idx
                        avlist.append((h, yb, co, w, a * 4 + r, st_flag, sp_flag))
                    pending.append((hp, pp, pvexp, mvs, avlist, vbuf))
                while pending:
                    emit_av(pending.pop(0))
                pv_all = sPE.n

                def make_norm(hp=hp, qg=qg, pv_all=pv_all, vbuf=vbuf,
                              rws=tuple(rw_set)):
                    def go():
                        sets = [(h, qc) for h in range(2) for qc in range(2)]
                        for si, (h, qc) in enumerate(sets):
                            yb = ybank[(h, qc)]
                            rbk = si % 2
                            E("tensor", lambda e, o=ps[rbk][0:64, 0:512],
                              l=small[64:65, 0:64], r2=ofin[64:65, si * 512:(si + 1) * 512]:
                              e.matmul(o, l, r2, start=True, stop=True),
                              waits=((sSC, rws[si]),) + bank_wait(rbk), incs=((sPE, 1),))
                            pv = sPE.n
                            E("vector", lambda e, o=t1s[0:64, 0:512], s=ps[rbk][0:64, 0:512]:
                              e.tensor_copy(o, s), waits=((sPE, pv),), incs=((sVE, 1),))
                            bank_consumed(rbk, sVE, sVE.n)
                            if h == 0:
                                dst = AY[0:64, hp * 2048 + qg * 1024 + qc * 512:
                                         hp * 2048 + qg * 1024 + qc * 512 + 512]
                            else:
                                dst = kvev[0:64, qc * 512:(qc + 1) * 512]
                            kw = ((dX, h2dma[0]),) if h == 1 else ()
                            E("vector", lambda e, o=dst, a2=ps[yb][0:64, 0:512],
                              b2=t1s[0:64, 0:512]:
                              e.tensor_mul(o, a2, b2), waits=kw, incs=((sVE, 1),))
                            bank_consumed(yb, sVE, sVE.n)
                        vv = sVE.n
                        E("scalar", lambda e,
                          o=AY[64:128, hp * 2048 + qg * 1024: hp * 2048 + qg * 1024 + 1024],
                          s=kvev[0:64, 0:1024]: e.dma_start(out=o, in_=s),
                          waits=((sVE, vv),), incs=((dX, 16),))
                        h2dma[0] = dX.n
                    return go
                norm_queue.append(make_norm())
            hp_mm_done[hp] = sPE.n
        flush_norm()
        E("tensor", None, waits=((dKV, dKV.n),))

    def P3_dbg():
        if STAGE != 3:
            return
        E("scalar", None, waits=((dKV, dKV.n), (dX, dX.n), (sVE, sVE.n)))
        for hp in range(8):
            E("scalar", lambda e, d=dbg["yT"][hp * 128:(hp + 1) * 128, :], s=AY[:, hp * 2048:(hp + 1) * 2048]:
              e.dma_start(out=d, in_=s), incs=((dX, 16),))
        E("scalar", lambda e, d=out_c[0:128, :], s=xstg[:, 0:1024]:
          e.dma_start(out=d, in_=s), incs=((dX, 16),))
        E("scalar", None, waits=((dX, dX.n),))

    def WPJ(ct):
        return AK[:, ct * 1024:(ct + 1) * 1024]

    def P4():
        for ft in range(8):
            buf = xstg[:, (ft % 2) * 1024:(ft % 2) * 1024 + 1024]
            wv = ((sPE, wp_done[ft - 2]),) if ft >= 2 else ()
            E("sync", lambda e, d=buf, s=W_proj[ft * 128:(ft + 1) * 128, :]:
              e.dma_start(out=d, in_=s), waits=wv, incs=((dKV, 16),))
            v = dKV.n
            for ct in range(8):
                bk = 6 + (ct % 2)
                bank = ps[bk]
                E("tensor", lambda e, o=bank[:, 0:128], i=buf[:, ct * 128:(ct + 1) * 128]:
                  e.transpose(o, i, identsb),
                  waits=((dKV, v),) + bank_wait(bk), incs=((sPE, 1),))
                pv = sPE.n
                E("vector", lambda e, d=WPJ(ct)[:, ft * 128:(ft + 1) * 128], s=bank[:, 0:128]:
                  e.tensor_copy(d, s), waits=((sPE, pv),), incs=((sVE, 1),))
                bank_consumed(bk, sVE, sVE.n)
            wp_done[ft] = sPE.n
        vfence()
        wpj_ready = sVE.n
        for ch in range(4):
            ev = {}
            for wave in range(2):
                stage_t = oev if wave == 0 else ofin
                for oti in range(4):
                    ot = 4 * wave + oti
                    bk = oti
                    for ct in range(8):
                        E("tensor", lambda e, o=ps[bk][:],
                          l=WPJ(ct)[:, ot * 128:(ot + 1) * 128],
                          r2=AY[:, ct * 2048 + ch * 512: ct * 2048 + (ch + 1) * 512],
                          st=(ct == 0), sp=(ct == 7):
                          e.matmul(o, l, r2, start=st, stop=sp),
                          waits=(((sVE, wpj_ready), (dX, h2dma[0])) + bank_wait(bk))
                                if ct == 0 else (),
                          incs=(((sPE, 1),) if ct == 7 else ()))
                    pv = sPE.n
                    E("scalar", lambda e, d=stage_t[:, oti * 512:(oti + 1) * 512],
                      s=ps[bk][:], bi=bproj_sb[:, ot:ot + 1]:
                      e.activation(d, s, AF.Identity, bias=bi, scale=1.0),
                      waits=((sPE, pv),), incs=((sSC, 1),))
                    bank_consumed(bk, sSC, sSC.n)
            sfence()
            ev_done2 = sSC.n
            for tt4 in range(4):
                tt = 4 * ch + tt4
                sl = tt4 % 2
                xb = xstg[:, sl * 1024:(sl + 1) * 1024]
                wv = ((dX, spill_done[tt - 2]),) if (tt - 2) in spill_done else ()
                E("scalar", lambda e, d=xb, s=x_own[tt * 128:(tt + 1) * 128, :]:
                  e.dma_start(out=d, in_=s), waits=wv, incs=((dX, 16),))
                vx = dX.n
                for ct in range(8):
                    stage_t = oev if ct < 4 else ofin
                    bk = 4 + ct // 4
                    E("tensor", lambda e, o=ps[bk][:, (ct % 4) * 128:(ct % 4) * 128 + 128],
                      i=stage_t[:, (ct % 4) * 512 + tt4 * 128:(ct % 4) * 512 + (tt4 + 1) * 128]:
                      e.transpose(o, i, identsb),
                      waits=((sSC, ev_done2),) + (bank_wait(bk) if ct % 4 == 0 else ()),
                      incs=((sPE, 1),))
                pv = sPE.n
                E("vector", lambda e, o=xb[:, 0:512], a2=ps[4][:], b2=xb[:, 0:512]:
                  e.tensor_add(o, a2, b2), waits=((sPE, pv), (dX, vx)))
                E("vector", lambda e, o=xb[:, 512:1024], a2=ps[5][:], b2=xb[:, 512:1024]:
                  e.tensor_add(o, a2, b2), incs=((sVE, 1),))
                bank_consumed(4, sVE, sVE.n)
                bank_consumed(5, sVE, sVE.n)
                av = sVE.n
                E("scalar", lambda e, d=x2sp[tt * 128:(tt + 1) * 128, :], s=xb:
                  e.dma_start(out=d, in_=s), waits=((sVE, av),), incs=((dX, 16),))
                spill_done[tt] = dX.n
        for tp in range(NT // 2):
            tA, tB = 2 * tp, 2 * tp + 1
            pg = p4_pg0 + tp
            xb = xstg if tp % 2 == 0 else xstg2
            lb = lnstg if tp % 2 == 0 else lnstg2
            xbA, xbB = xb[:, 0:1024], xb[:, 1024:2048]
            lbA, lbB = lb[:, 0:1024], lb[:, 1024:2048]
            w = ((sVE, tr_done[pg - 2]),) if tp >= 2 else \
                ((dX, spill_done[15]),)
            E("scalar", lambda e, d=xbA, s=x2sp[tA * 128:(tA + 1) * 128, :]:
              e.dma_start(out=d, in_=s), waits=w, incs=((dX, 16),))
            E("scalar", lambda e, d=xbB, s=x2sp[tB * 128:(tB + 1) * 128, :]:
              e.dma_start(out=d, in_=s), incs=((dX, 16),))
            vx = dX.n
            sc = emit_ln_pair(xbA, lbA, xbB, lbB, C, wait_x=((dX, vx),), war_wait=w)
            emit_transpose_cast(lbA, lambda cp, t2=tA: XL2p(cp, t2), sc)
            tr_done[pg] = emit_transpose_cast(
                lbB, lambda cp, t2=tB: XL2p(cp, t2), sc)

    def XL2(ct):
        return AQ[:, ct * 2048:(ct + 1) * 2048]

    def P4_dbg():
        if STAGE != 4:
            return
        E("scalar", None, waits=((sVE, tr_done[p4_pg0 + 7]), (dX, dX.n)))
        E("scalar", lambda e, d=dbg["x2"], s=x2sp: e.dma_start(out=d, in_=s),
          incs=((dX, 16),))
        for ct in range(8):
            E("scalar", lambda e, d=dbg["xln2T"][ct * 128:(ct + 1) * 128, :], s=XL2(ct):
              e.dma_start(out=d, in_=s), incs=((dX, 16),))
        E("scalar", lambda e, d=out_c[0:128, :], s=xstg[:, 0:1024]:
          e.dma_start(out=d, in_=s), incs=((dX, 16),))
        E("scalar", None, waits=((dX, dX.n),))

    def P5():
        gw = kvev[:, 0:128].rearrange("p (ct e2) -> p ct e2", ct=8)
        E("gpsimd", lambda e, d=gw, s=gate_w.rearrange("(ct p) e2 -> p ct e2", p=128):
          e.dma_start(out=d, in_=s),
          waits=((dKV, dKV.n), (sVE, sVE.n)), incs=((dW, 16),))
        gwv = dW.n
        for ch in range(4):
            bk = ch % 2
            for ct in range(8):
                E("tensor", lambda e, o=ps[bk][0:16, 0:512], l=gw[:, ct, :],
                  r2=XL2(ct)[:, ch * 512:(ch + 1) * 512], st=(ct == 0), sp=(ct == 7):
                  e.matmul(o, l, r2, start=st, stop=sp),
                  waits=(((dW, gwv), (sVE, tr_done[p4_pg0 + 7])) + bank_wait(bk))
                        if ct == 0 else (),
                  incs=(((sPE, 1),) if ct == 7 else ()))
                pv = sPE.n
            E("scalar", lambda e, d=rT16[0:16, ch * 512:(ch + 1) * 512],
              s=ps[bk][0:16, 0:512], bi=gateb_sb[0:16, 0:1]:
              e.activation(d, s, AF.Sigmoid, bias=bi, scale=1.0),
              waits=((sPE, pv),), incs=((sSC, 1),))
            bank_consumed(bk, sSC, sSC.n)
        sfence()
        rfv = sSC.n
        for tg in range(4):
            tls = [4 * tg + i for i in range(4)]
            for i, t in enumerate(tls):
                bk = 2 + i % 2
                E("tensor", lambda e, o=ps[bk][0:128, 0:16],
                  i2=rT16[0:16, t * 128:(t + 1) * 128], idn=identsb[0:16, 0:16]:
                  e.transpose(o, i2, idn),
                  waits=((sSC, rfv),) + bank_wait(bk), incs=((sPE, 1),))
                pv = sPE.n
                E("vector", lambda e, o=rtok[:, 16 * i:16 * i + 16], s=ps[bk][0:128, 0:16]:
                  e.tensor_copy(o, s), waits=((sPE, pv),), incs=((sVE, 1),))
                bank_consumed(bk, sVE, sVE.n)
            for i in range(4):
                E("vector", lambda e, o=small[:, 64 + i:65 + i], s=rtok[:, 16 * i:16 * i + 16]:
                  e.reduce_max(o, s, axis=AX))
            for i in range(4):
                E("vector", lambda e, o=wsl[:, 16 * i:16 * i + 16],
                  s=rtok[:, 16 * i:16 * i + 16], m=small[:, 64 + i:65 + i]:
                  e.tensor_scalar(o, s, m, None, ALU.is_equal))
            for i in range(4):
                E("vector", lambda e, o=r2sl[:, 16 * i:16 * i + 16],
                  a2=wsl[:, 16 * i:16 * i + 16], s=rtok[:, 16 * i:16 * i + 16]:
                  e.scalar_tensor_tensor(o, a2, -1e30, s, ALU.mult, ALU.add))
            for i in range(4):
                E("vector", lambda e, o=small[:, 68 + i:69 + i], s=r2sl[:, 16 * i:16 * i + 16]:
                  e.reduce_max(o, s, axis=AX))
            for i in range(4):
                E("vector", lambda e, o=r2sl[:, 16 * i:16 * i + 16],
                  s=r2sl[:, 16 * i:16 * i + 16], m=small[:, 68 + i:69 + i]:
                  e.tensor_scalar(o, s, m, None, ALU.is_equal))
            for i in range(4):
                E("vector", lambda e, o=wsl[:, 16 * i:16 * i + 16],
                  a2=wsl[:, 16 * i:16 * i + 16], b2=r2sl[:, 16 * i:16 * i + 16]:
                  e.tensor_add(o, a2, b2))
            for i in range(4):
                E("vector", lambda e, o=small[:, 72 + i:73 + i],
                  a2=small[:, 64 + i:65 + i], b2=small[:, 68 + i:69 + i]:
                  e.tensor_add(o, a2, b2))
            vfence()
            vv = sVE.n
            for i in range(4):
                E("scalar", lambda e, o=small[:, 76 + i:77 + i], s=small[:, 72 + i:73 + i]:
                  e.activation(o, s, AF.Ln, bias=0.0, scale=1.0), waits=((sVE, vv),))
            for i in range(4):
                E("scalar", lambda e, o=small[:, 76 + i:77 + i], s=small[:, 76 + i:77 + i]:
                  e.mul(o, s, -1.0))
            for i in range(4):
                E("scalar", lambda e, o=small[:, 80 + i:81 + i], s=small[:, 76 + i:77 + i]:
                  e.activation(o, s, AF.Exp, bias=0.0, scale=1.0), incs=((sSC, 1),))
            sfence()
            sv = sSC.n
            for i in range(4):
                E("vector", lambda e, o=wsl[:, 16 * i:16 * i + 16],
                  a2=wsl[:, 16 * i:16 * i + 16], b2=rtok[:, 16 * i:16 * i + 16]:
                  e.tensor_mul(o, a2, b2), waits=((sSC, sv),))
            for i in range(4):
                E("vector", lambda e, o=wsl[:, 16 * i:16 * i + 16],
                  s=wsl[:, 16 * i:16 * i + 16], m=small[:, 80 + i:81 + i]:
                  e.tensor_scalar(o, s, m, None, ALU.mult), incs=((sVE, 1),))
            vfence()
            wv2 = sVE.n
            for i, t in enumerate(tls):
                bk = 2 + i % 2
                E("tensor", lambda e, o=ps[bk][0:16, 0:128],
                  i2=wsl[:, 16 * i:16 * i + 16], idn=identsb:
                  e.transpose(o, i2, idn),
                  waits=((sVE, wv2),) + bank_wait(bk), incs=((sPE, 1),))
                pv = sPE.n
                E("vector", lambda e, o=wT16[0:16, t * 128:(t + 1) * 128], s=ps[bk][0:16, 0:128]:
                  e.tensor_copy(o, s), waits=((sPE, pv),), incs=((sVE, 1),))
                bank_consumed(bk, sVE, sVE.n)
        vfence()

    def P5_dbg():
        if STAGE != 5:
            return
        E("scalar", None, waits=((sVE, sVE.n),))
        E("scalar", lambda e, d=dbg["wT"], s=wT16[0:16, :]: e.dma_start(out=d, in_=s),
          incs=((dX, 16),))
        E("scalar", lambda e, d=dbg["rT"], s=rT16[0:16, :]: e.dma_start(out=d, in_=s),
          incs=((dX, 16),))
        E("scalar", lambda e, d=dbg["rtok"], s=rtok[:, 0:64]: e.dma_start(out=d, in_=s),
          incs=((dX, 16),))
        E("scalar", lambda e, d=out_c[0:128, :], s=xstg[:, 0:1024]:
          e.dma_start(out=d, in_=s), incs=((dX, 16),))
        E("scalar", None, waits=((dX, dX.n),))

    def P6():
        E("gpsimd", lambda e, d=eT[0:16, 0:2048], s=selb: e.dma_start(out=d, in_=s),
          waits=((sPE, sPE.n),), incs=((dW, 16),))
        selv = dW.n
        comb_guard = {}
        out_done = {}
        wload = {}
        wfree = {-2: sPE.n, -1: sPE.n}
        hp_free = {}
        last_t1 = [0]
        ffw_free = {}
        acc_add = {}
        stg_free = {}

        def emit_combine(ch):
            c0 = ch * 256
            slot = ch % 2
            stg = oev if slot == 0 else ofin
            sw = comb_guard.get(slot, ())
            E("sync", lambda e,
              d=stg[:, 0:2048].rearrange("p (ot t2) -> p ot t2", ot=8),
              s=ffacc.rearrange("(ot p) t2 -> p ot t2", p=128)[:, :, c0:c0 + 256]:
              e.dma_start(out=d, in_=s),
              waits=((dW, acc_add[(16, ch)]),) + sw, incs=((dKV, 16),))
            lv = dKV.n
            for ot in range(8):
                E("vector", lambda e, o=stg[:, ot * 256:(ot + 1) * 256],
                  a2=stg[:, ot * 256:(ot + 1) * 256], b2=XL2(ot)[:, c0:c0 + 256]:
                  e.tensor_add(o, a2, b2),
                  waits=((dKV, lv),), incs=(((sVE, 1),) if ot == 7 else ()))
            vfence()
            adv = sVE.n
            for tt2 in range(2):
                tt = 2 * ch + tt2
                sl = tt2
                xb = xstg[:, sl * 1024:(sl + 1) * 1024]
                wv2 = ((dX, out_done[tt - 2]),) if (tt - 2) in out_done else ()
                E("scalar", lambda e, d=xb, s=x2sp[tt * 128:(tt + 1) * 128, :]:
                  e.dma_start(out=d, in_=s), waits=wv2, incs=((dX, 16),))
                vx = dX.n
                for ct in range(8):
                    bk = 4 + ct // 4
                    E("tensor", lambda e, o=ps[bk][:, (ct % 4) * 128:(ct % 4) * 128 + 128],
                      i=stg[:, ct * 256 + tt2 * 128: ct * 256 + (tt2 + 1) * 128]:
                      e.transpose(o, i, identsb),
                      waits=((sVE, adv),) + (bank_wait(bk) if ct % 4 == 0 else ()),
                      incs=((sPE, 1),))
                pv = sPE.n
                fb = hprime[:, 0:1024].rearrange("p x -> p x") if False else None
                E("vector", lambda e, o=xb[:, 0:512], a2=ps[4][:], b2=xb[:, 0:512]:
                  e.tensor_add(o, a2, b2), waits=((sPE, pv), (dX, vx)))
                E("vector", lambda e, o=xb[:, 512:1024], a2=ps[5][:], b2=xb[:, 512:1024]:
                  e.tensor_add(o, a2, b2), incs=((sVE, 1),))
                bank_consumed(4, sVE, sVE.n)
                bank_consumed(5, sVE, sVE.n)
                fv = sVE.n
                E("scalar", lambda e, d=out_c[tt * 128:(tt + 1) * 128, :], s=xb:
                  e.dma_start(out=d, in_=s), waits=((sVE, fv),), incs=((dX, 16),))
                out_done[tt] = dX.n
            comb_guard[slot] = ((sPE, pv),)

        for e_ in range(17):
            par = e_ % 2
            base = par * 12288
            wv = ((sPE, wfree.get(e_ - 2, 0)), (dW, cast_dw[16]))
            for pi in range(3):
                off = (0, 4096, 8192)[pi]
                E("sync", lambda e, d=AW[:, base + off:base + off + 4096], s=wslab[e_, pi]:
                  e.dma_start(out=d, in_=s),
                  waits=wv if pi == 0 else (), incs=((dKV, 16),))
            wload[e_] = dKV.n
            if e_ > 0:
                fsl = ffw[:, (e_ % 2) * 2048:(e_ % 2) * 2048 + 2048]
                for ch in range(8):
                    bk = 4 + ch % 2
                    E("tensor", lambda e, o=ps[bk][0:128, 0:256],
                      l=eT[0:16, (e_ - 1) * 128:e_ * 128],
                      r2=wT16[0:16, ch * 256:(ch + 1) * 256]:
                      e.matmul(o, l, r2, start=True, stop=True),
                      waits=((dW, selv), (sVE, ffw_free.get(e_ - 2, 0))) + bank_wait(bk),
                      incs=((sPE, 1),))
                    pv = sPE.n
                    E("vector", lambda e, o=fsl[:, ch * 256:(ch + 1) * 256], s=ps[bk][0:128, 0:256]:
                      e.tensor_copy(o, s), waits=((sPE, pv),), incs=((sVE, 1),))
                    bank_consumed(bk, sVE, sVE.n)
                vfence()
                ffw_free[e_] = sVE.n
            for ch in range(8):
                c0 = ch * 256
                pvgu = {}
                for hf in range(2):
                    gb, ub = (4, 5) if hf == 0 else (6, 7)
                    for mt in range(2):
                        for ct in range(8):
                            E("tensor", lambda e, o=ps[gb][:, mt * 256:(mt + 1) * 256],
                              l=AW[:, base + ct * 512 + hf * 256 + mt * 128:
                                    base + ct * 512 + hf * 256 + mt * 128 + 128],
                              r2=XL2(ct)[:, c0:c0 + 256], st=(ct == 0), sp=(ct == 7):
                              e.matmul(o, l, r2, start=st, stop=sp),
                              waits=(((dKV, wload[e_]),) + bank_wait(gb)) if (mt == 0 and ct == 0) else (),
                              incs=())
                    for mt in range(2):
                        for ct in range(8):
                            E("tensor", lambda e, o=ps[ub][:, mt * 256:(mt + 1) * 256],
                              l=AW[:, base + 4096 + ct * 512 + hf * 256 + mt * 128:
                                    base + 4096 + ct * 512 + hf * 256 + mt * 128 + 128],
                              r2=XL2(ct)[:, c0:c0 + 256], st=(ct == 0), sp=(ct == 7):
                              e.matmul(o, l, r2, start=st, stop=sp),
                              waits=(bank_wait(ub)) if (mt == 0 and ct == 0) else (),
                              incs=(((sPE, 1),) if (mt == 1 and ct == 7) else ()))
                    pvgu[hf] = sPE.n
                for hf in range(2):
                    gb, ub = (4, 5) if hf == 0 else (6, 7)
                    pv = pvgu[hf]
                    hw = ((sPE, hp_free.get((e_, ch - 1, hf), hp_free.get((e_ - 1, 7, hf), 0))),)
                    E("scalar", lambda e, o=t1s[:, 0:512], s=ps[gb][:]:
                      e.activation(o, s, AF.Silu, bias=0.0, scale=1.0),
                      waits=((sPE, pv), (sVE, last_t1[0])) + hw, incs=((sSC, 1),))
                    scv = sSC.n
                    hslot = hprime[:, hf * 512:(hf + 1) * 512]
                    E("vector", lambda e, o=hslot, a2=t1s[:, 0:512], b2=ps[ub][:]:
                      e.tensor_mul(o, a2, b2), waits=((sSC, scv),) + hw)
                    bank_consumed(gb, sVE, sVE.n + 1)
                    bank_consumed(ub, sVE, sVE.n + 1)
                    if e_ > 0:
                        fsl = ffw[:, (e_ % 2) * 2048:(e_ % 2) * 2048 + 2048]
                        E("vector", lambda e, o=hslot[:, 0:256], a2=hslot[:, 0:256],
                          b2=fsl[:, c0:c0 + 256]:
                          e.tensor_mul(o, a2, b2), waits=((sVE, ffw_free[e_]),))
                        E("vector", lambda e, o=hslot[:, 256:512], a2=hslot[:, 256:512],
                          b2=fsl[:, c0:c0 + 256]:
                          e.tensor_mul(o, a2, b2), incs=((sVE, 1),))
                    else:
                        E("vector", lambda e, o=hslot[:, 0:1], a2=hslot[:, 0:1],
                          b2=hslot[:, 0:1]: e.tensor_copy(o, a2), incs=((sVE, 1),))
                    vfence()
                    hv = sVE.n
                    last_t1[0] = hv
                    for ct2 in range(2):
                        for ot in range(8):
                            st_flag = (hf == 0 and ct2 == 0)
                            sp_flag = (hf == 1 and ct2 == 1)
                            obk = ps[ot // 2][:, (ot % 2) * 256:(ot % 2) * 256 + 256]
                            E("tensor", lambda e, o=obk,
                              l=AW[:, base + 8192 + (2 * hf + ct2) * 1024 + ot * 128:
                                    base + 8192 + (2 * hf + ct2) * 1024 + ot * 128 + 128],
                              r2=hslot[:, ct2 * 256:(ct2 + 1) * 256],
                              st=(st_flag and ot % 2 == 0), sp=sp_flag:
                              e.matmul(o, l, r2, start=st, stop=sp, skip_group_check=True),
                              waits=((sVE, hv),) + (bank_wait(ot // 2) if (hf == 0 and ct2 == 0 and ot % 2 == 0) else ()),
                              incs=((sPE, 1),) if (ct2 == 1 and ot == 7) else ())
                    hp_free[(e_, ch, hf)] = sPE.n
                pv2 = sPE.n
                if e_ == 16 and ch >= 2:
                    emit_combine(ch - 2)
                slot = (e_ * 8 + ch) % 2
                stg = oev if slot == 0 else ofin
                sw = ((dW, stg_free.get(slot, 0)),) + comb_guard.get(slot, ())
                for bb in range(4):
                    E("vector", lambda e, o=stg[:, bb * 512:(bb + 1) * 512], s=ps[bb][:]:
                      e.tensor_copy(o, s),
                      waits=((sPE, pv2),) + sw, incs=(((sVE, 1),) if bb == 3 else ()))
                    bank_consumed(bb, sVE, sVE.n + (1 if bb < 3 else 0))
                vfence()
                ev = sVE.n
                aw2 = ((dW, acc_add.get((e_ - 1, ch), 0)),)
                E("gpsimd", lambda e,
                  d=ffacc.rearrange("(ot p) t2 -> p ot t2", p=128)[:, :, c0:c0 + 256],
                  s=stg[:, 0:2048].rearrange("p (ot t2) -> p ot t2", ot=8),
                  op=(ALU.add if e_ > 0 else ALU.bypass):
                  e.dma_start(out=d, in_=s, accum_op=op),
                  waits=((sVE, ev),) + aw2, incs=((dW, 16),))
                acc_add[(e_, ch)] = dW.n
                stg_free[slot] = dW.n
            wfree[e_] = sPE.n
        for ch in range(6, 8):
            emit_combine(ch)
        E("scalar", None, waits=((dX, dX.n),))

    P0()
    P0_wattn()
    P1()
    P1_dbg()
    if STAGE >= 2:
        P2()
        P0c()
        P2_dbg()
    if STAGE >= 3:
        P3()
        P3_dbg()
    if STAGE >= 4:
        P4()
        P4_dbg()
    if STAGE >= 5:
        P5()
        P5_dbg()
    if STAGE >= 6:
        P6()

    block = es.enter_context(nc.Block())
    with es:
        p.finalize(block)
    return nc


def make_inputs(inputs):
    x = np.asarray(inputs["L0_attn_x"], dtype=np.float32)
    masks_all = []
    in_maps = []
    shared = dict(
        W_attn=np.asarray(inputs["L0_attn_W_attn"], np.float32),
        b_attn=np.asarray(inputs["L0_attn_b_attn"], np.float32),
        W_proj=np.asarray(inputs["L0_attn_W_proj"], np.float32),
        b_proj=np.asarray(inputs["L0_attn_b_proj"], np.float32),
        gate_w=np.asarray(inputs["L0_ffn_gate_w"], np.float32),
        gate_b=np.asarray(inputs["L0_ffn_gate_b"], np.float32),
        sh_wg=np.asarray(inputs["L0_ffn_shared_wg"], np.float32),
        sh_wu=np.asarray(inputs["L0_ffn_shared_wu"], np.float32),
        sh_wd=np.asarray(inputs["L0_ffn_shared_wd"], np.float32),
        ex_wg=np.asarray(inputs["L0_ffn_experts_wg"], np.float32),
        ex_wu=np.asarray(inputs["L0_ffn_experts_wu"], np.float32),
        ex_wd=np.asarray(inputs["L0_ffn_experts_wd"], np.float32),
        identf=np.eye(128, dtype=np.float32),
    )
    for c in range(8):
        b, m = divmod(c, 4)
        o = 384 - 128 * m
        x_own = np.concatenate([x[b, 512 * t + o: 512 * t + o + 128] for t in range(NT)])
        kk = np.arange(128)[:, None]
        qq = np.arange(128)[None, :]
        mk = np.stack([( (128 * r + kk) <= (o + qq)).astype(BF) for r in range(4)])
        im = dict(shared)
        im["x_own"] = np.ascontiguousarray(x_own)
        im["selb"] = np.kron(np.eye(16, dtype=np.float32), np.ones((1, 128), np.float32)).astype(BF)
        im["x_kv"] = np.ascontiguousarray(x[b])
        im["masks"] = mk
        in_maps.append(im)
    return in_maps


def kernel(**inputs):
    in_maps = make_inputs(inputs)
    nc = build_nc()
    res = run_bass_kernel_spmd(nc, in_maps, core_ids=list(range(8)))
    x = np.asarray(inputs["L0_attn_x"], np.float32)
    out = np.empty_like(x)
    for c in range(8):
        b, m = divmod(c, 4)
        o = 384 - 128 * m
        oc = res.results[c]["out_c"]
        for t in range(NT):
            out[b, 512 * t + o: 512 * t + o + 128] = oc[128 * t: 128 * (t + 1)]
    return out


def debug_run(inputs):
    in_maps = make_inputs(inputs)
    nc = build_nc()
    res = run_bass_kernel_spmd(nc, in_maps, core_ids=list(range(8)))
    return res, in_maps


# version 16
# speedup vs baseline: 1.1639x; 1.1639x over previous
import contextlib
import os
import numpy as np
import ml_dtypes
import concourse.bass as bass
import concourse.mybir as mybir
from concourse.bass_utils import run_bass_kernel_spmd

F32 = mybir.dt.float32
BF16 = mybir.dt.bfloat16
BF = ml_dtypes.bfloat16
AF = mybir.ActivationFunctionType
ALU = mybir.AluOpType
AX = mybir.AxisListType.X

B, T, C, H, HD = 2, 8192, 1024, 16, 64
NT, QT = 16, 128
TOK = NT * QT
NHP = H // 2
NE, I_FF = 16, 512
EXP_BIAS = -12.0
EPS = 1e-5
STAGE = 6

ENGS = ("sync", "scalar", "vector", "tensor", "gpsimd")


class Sem:
    def __init__(self, nc, es, name):
        self.h = es.enter_context(nc.semaphore(name))
        self.n = 0
        self.name = name

    def bump(self, k):
        self.n += k
        return self.n


class Prog:
    def __init__(self, nc, es):
        self.nc = nc
        self.es = es
        self.stream = {e: [] for e in ENGS}
        self.hw = {e: {} for e in ENGS}

    def sem(self, name):
        return Sem(self.nc, self.es, name)

    def emit(self, eng, builder, waits=(), incs=()):
        waits = tuple(waits)
        for s, k in incs:
            if k == 16:
                waits = ((s, s.n),) + waits
        wl = []
        for s, v in waits:
            if self.hw[eng].get(s.name, -1) < v:
                self.hw[eng][s.name] = v
                wl.append((s.h, v))
        il = []
        for s, k in incs:
            s.bump(k)
            il.append((s.h, k, s.n))
        self.stream[eng].append((builder, wl, il))

    def wait(self, eng, sem, val):
        self.emit(eng, None, waits=((sem, val),))

    def finalize(self, block):
        def mk(eng_name):
            def run(engine):
                for builder, wl, il in self.stream[eng_name]:
                    for h, v in wl:
                        engine.wait_ge(h, v)
                    if builder is None:
                        continue
                    inst = builder(engine)
                    for h, k, _ in il:
                        inst.then_inc(h, k)
            return run
        block.sync(mk("sync"))
        block.scalar(mk("scalar"))
        block.vector(mk("vector"))
        block.tensor(mk("tensor"))
        block.gpsimd(mk("gpsimd"))


def build_nc():
    nc = bass.Bass("TRN2", target_bir_lowering=False, detect_race_conditions=False)
    es = contextlib.ExitStack()
    p = Prog(nc, es)

    def din(name, shape, dt=F32):
        return nc.dram_tensor(name, shape, dt, kind="ExternalInput").ap()

    x_own = din("x_own", [TOK, C])
    x_kv = din("x_kv", [T, C])
    masks = din("masks", [4, 128, 128], BF16)
    identf = din("identf", [128, 128], F32)
    W_attn = din("W_attn", [3 * C, C])
    b_attn = din("b_attn", [3 * C])
    W_proj = din("W_proj", [C, C])
    b_proj = din("b_proj", [C])
    gate_w = din("gate_w", [C, NE])
    gate_b = din("gate_b", [NE])
    sh_wg = din("sh_wg", [C, I_FF])
    sh_wu = din("sh_wu", [C, I_FF])
    sh_wd = din("sh_wd", [I_FF, C])
    ex_wg = din("ex_wg", [NE, C, I_FF])
    ex_wu = din("ex_wu", [NE, C, I_FF])
    ex_wd = din("ex_wd", [NE, I_FF, C])
    selb = din("selb", [16, 2048], BF16)

    out_c = nc.dram_tensor("out_c", [TOK, C], F32, kind="ExternalOutput").ap()
    dbg = {}
    if STAGE == 1:
        dbg["qT"] = nc.dram_tensor("dbg_qT", [C, TOK], BF16, kind="ExternalOutput").ap()
        dbg["xlnT"] = nc.dram_tensor("dbg_xlnT", [C, TOK], BF16, kind="ExternalOutput").ap()
        dbg["st"] = nc.dram_tensor("dbg_st", [128, 64], F32, kind="ExternalOutput").ap()
    if STAGE == 2:
        dbg["kslab"] = nc.dram_tensor("dbg_kslab", [NHP, 128, T], BF16, kind="ExternalOutput").ap()
        dbg["vslab"] = nc.dram_tensor("dbg_vslab", [NHP, T, 130], BF16, kind="ExternalOutput").ap()
    if STAGE == 3:
        dbg["yT"] = nc.dram_tensor("dbg_yT", [C, TOK], BF16, kind="ExternalOutput").ap()
    if STAGE == 4:
        dbg["x2"] = nc.dram_tensor("dbg_x2", [TOK, C], F32, kind="ExternalOutput").ap()
        dbg["xln2T"] = nc.dram_tensor("dbg_xln2T", [C, TOK], BF16, kind="ExternalOutput").ap()
    if False:
        dbg["acc0"] = nc.dram_tensor("dbg_acc0", [128, 2048], F32, kind="ExternalOutput").ap()
        dbg["acc1"] = nc.dram_tensor("dbg_acc1", [128, 2048], F32, kind="ExternalOutput").ap()
        dbg["ffw"] = nc.dram_tensor("dbg_ffw", [128, 4096], BF16, kind="ExternalOutput").ap()
        dbg["hp0"] = nc.dram_tensor("dbg_hp0", [128, 1024], BF16, kind="ExternalOutput").ap()
        dbg["oev"] = nc.dram_tensor("dbg_oev", [128, 2048], F32, kind="ExternalOutput").ap()
    if STAGE == 5:
        dbg["wT"] = nc.dram_tensor("dbg_wT", [NE, TOK], BF16, kind="ExternalOutput").ap()
        dbg["rT"] = nc.dram_tensor("dbg_rT", [NE, TOK], F32, kind="ExternalOutput").ap()
        dbg["rtok"] = nc.dram_tensor("dbg_rtok", [128, 64], F32, kind="ExternalOutput").ap()

    kslab = nc.dram_tensor("kslab", [NHP, 128, T], BF16, kind="Internal").ap()
    vslab = nc.dram_tensor("vslab", [NHP, T, 130], BF16, kind="Internal").ap()
    x2sp = nc.dram_tensor("x2sp", [TOK, C], F32, kind="Internal").ap()
    wslab = nc.dram_tensor("wslab", [17, 3, 128, 4096], BF16, kind="Internal").ap()
    ffacc = nc.dram_tensor("ffacc", [C, TOK], F32, kind="Internal").ap()

    def sb(name, shape, dt):
        return es.enter_context(nc.sbuf_tensor(name, shape, dt))

    AW = sb("AW", [128, 24576], BF16)
    AQ = sb("AQ", [128, 16384], BF16)
    AY = sb("AY", [128, 16384], BF16)
    AK = sb("AK", [128, 16512], BF16)
    xstg = sb("xstg", [128, 2048], F32)
    lnstg = sb("lnstg", [128, 2048], BF16)
    xstg2 = sb("xstg2", [128, 2048], F32)
    lnstg2 = sb("lnstg2", [128, 2048], BF16)
    identbf = sb("identbf", [128, 128], BF16)
    stats = sb("stats", [128, 16], F32)
    xlk = sb("xlk", [128, 4096], BF16)
    kvev = sb("kvev", [128, 1024], BF16)
    eT = sb("eT", [128, 2048], BF16)
    vst = sb("vst", [128, 2080], BF16)
    consts = sb("consts", [128, 136], F32)
    cbf = sb("cbf", [128, 512], BF16)
    rowbf = sb("rowbf", [1, 1152], BF16)
    rowf = sb("rowf", [1, 128], F32)
    small = sb("small", [128, 128], F32)
    hprime = sb("hprime", [128, 1024], BF16)
    t1s = sb("t1s", [128, 512], F32)
    oev = sb("oev", [128, 2048], F32)
    rtok = sb("rtok", [128, 64], F32)
    wsl = sb("wsl", [128, 64], F32)
    r2sl = sb("r2sl", [128, 64], F32)
    wstg = oev
    sqscr = eT[:, 0:1024]
    ffw = nc.virtual_sbuf_tensor("ffw", [128, 4096], BF16)
    nc.back_virtual_tensor_with(ffw, xlk)
    wT16 = vst[0:16, 0:2048]
    ofin = xstg2
    rT16 = oev[0:16, :]

    battn_sb = sb("battn", [128, 24], F32)
    bproj_sb = sb("bproj", [128, 8], F32)
    gateb_sb = sb("gateb", [16, 1], F32)

    identsb = consts[:, 0:128]
    expbias = consts[:, 128:129]
    lnepsb = consts[:, 129:130]
    halflnb = consts[:, 130:131]

    psD = [es.enter_context(nc.psum_tensor(f"psD{i}", [128, 1024], F32)) for i in range(4)]
    ps = [psD[i // 2][:, (i % 2) * 512:(i % 2) * 512 + 512] for i in range(8)]
    psg = {}

    def bank_wait(i):
        g = psg.get(i)
        return (g,) if g else ()

    def bank_consumed(i, sem, val):
        psg[i] = (sem, val)

    def vfence():
        E("vector", lambda e: e.tensor_copy(stats[:, 7:8], consts[:, 128:129]),
          incs=((sVE, 1),))
        return sVE.n

    def sfence():
        E("scalar", lambda e: e.copy(stats[:, 7:8], consts[:, 128:129]),
          incs=((sSC, 1),))
        return sSC.n

    dW = p.sem("dW")
    dX = p.sem("dX")
    dKV = p.sem("dKV")
    sPE = p.sem("sPE")
    sSC = p.sem("sSC")
    sVE = p.sem("sVE")
    sGP = p.sem("sGP")

    E = p.emit

    identbf_ready = [0]
    h2dma = [0]
    h2_byqg = {0: 0, 1: 0}

    def P0():
        E("sync", lambda e, d=identsb, s=identf: e.dma_start(out=d, in_=s), incs=((dKV, 16),))
        E("sync", lambda e, d=cbf[:, 0:512].rearrange("p (r q) -> p r q", r=4),
          s=masks.rearrange("r kk qq -> kk r qq"): e.dma_start(out=d, in_=s), incs=((dKV, 16),))
        E("sync", lambda e, d=battn_sb[:], s=b_attn.rearrange("(t p) -> p t", p=128):
          e.dma_start(out=d, in_=s, allow_slow_non_contiguous=True), incs=((dKV, 16),))
        E("sync", lambda e, d=bproj_sb[:], s=b_proj.rearrange("(t p) -> p t", p=128):
          e.dma_start(out=d, in_=s, allow_slow_non_contiguous=True), incs=((dKV, 16),))
        E("sync", lambda e, d=gateb_sb[:], s=gate_b.rearrange("(a b) -> a b", b=1):
          e.dma_start(out=d, in_=s, allow_slow_non_contiguous=True), incs=((dKV, 16),))
        E("gpsimd", lambda e, d=rowbf[0:1, 0:1024], s=b_attn[2048:3072].rearrange("(a b) -> a b", a=1):
          e.dma_start(out=d, in_=s), incs=((dW, 16),))
        E("vector", lambda e: e.memset(expbias, EXP_BIAS), incs=((sVE, 1),))
        E("vector", lambda e: e.memset(lnepsb, EPS * 1024.0))
        E("vector", lambda e: e.memset(halflnb, 0.5 * float(np.log(1024.0))))
        E("vector", lambda e: e.memset(rowbf[0:1, 1024:1152], 1.0))
        E("vector", lambda e: e.memset(rowf[0:1, 0:128], 1.0))
        E("vector", lambda e: e.memset(vst[:], 0.0))
        for bb in range(2):
            base = 1040 * bb
            E("vector", lambda e, b2=base: e.memset(
                vst[:, b2:b2 + 1040].rearrange("p (hp c) -> p hp c", hp=8)[:, :, 64:65], 1.0),
              incs=((sVE, 1),))
            E("vector", lambda e, b2=base: e.memset(
                vst[:, b2:b2 + 1040].rearrange("p (hp c) -> p hp c", hp=8)[:, :, 129:130], 1.0),
              incs=((sVE, 1),))
        E("vector", lambda e: e.memset(small[:, 0:64], 1.0), incs=((sVE, 1),))
        E("vector", lambda e: e.tensor_copy(identbf[:], identsb), waits=((dKV, 16),))
        identbf_ready[0] = vfence()

    cast_dw = {}

    def P0c():
        pace = ((dKV, dKV.n + (288 if STAGE >= 3 else 0)),)
        for e_ in range(17):
            if e_ == 0:
                srcs = (sh_wg, sh_wu, sh_wd)
            else:
                srcs = (ex_wg[e_ - 1], ex_wu[e_ - 1], ex_wd[e_ - 1])
            E("gpsimd", lambda e, d=wslab[e_, 0], s=srcs[0].rearrange("(ct p) i -> p ct i", p=128):
              e.dma_start(out=d.rearrange("p (ct i) -> p ct i", ct=8), in_=s),
              waits=(pace if e_ == 0 else ()), incs=((dW, 16),))
            E("gpsimd", lambda e, d=wslab[e_, 1], s=srcs[1].rearrange("(ct p) i -> p ct i", p=128):
              e.dma_start(out=d.rearrange("p (ct i) -> p ct i", ct=8), in_=s), incs=((dW, 16),))
            E("gpsimd", lambda e, d=wslab[e_, 2], s=srcs[2].rearrange("(ct p) d2 -> p ct d2", p=128):
              e.dma_start(out=d.rearrange("p (ct d2) -> p ct d2", ct=4), in_=s), incs=((dW, 16),))
            cast_dw[e_] = dW.n

    def WA(ct):
        return AW[:, ct * 3072:(ct + 1) * 3072]

    def P0_wattn():
        for ft in range(24):
            buf = wstg[:, (ft % 2) * 1024:(ft % 2) * 1024 + 1024]
            if ft >= 2:
                E("sync", None, waits=((sPE, p_done[ft - 2]),))
            E("sync", lambda e, d=buf, s=W_attn[ft * 128:(ft + 1) * 128, :]:
              e.dma_start(out=d, in_=s), incs=((dKV, 16),))
            v = dKV.n
            for ct in range(8):
                bk = ct % 2
                bank = ps[bk]
                E("tensor", lambda e, o=bank[:, 0:128], i=buf[:, ct * 128:(ct + 1) * 128]:
                  e.transpose(o, i, identsb),
                  waits=((dKV, v),) + bank_wait(bk), incs=((sPE, 1),))
                pe_v = sPE.n
                E("vector", lambda e, d=WA(ct)[:, ft * 128:(ft + 1) * 128], s=bank[:, 0:128]:
                  e.tensor_copy(d, s), waits=((sPE, pe_v),), incs=((sVE, 1),))
                bank_consumed(bk, sVE, sVE.n)
            p_done[ft] = sPE.n

    p_done = {}
    ev_done = {}

    ln_prev = {}
    def emit_ln_pair(xA, oA, xB, oB, n, wait_x=(), war_wait=()):
        stA = stats[:, 0:7]
        stB = stats[:, 8:15]
        wx = tuple(wait_x) + tuple(war_wait) + ((sSC, ln_prev.get(0, 0)),)
        E("scalar", lambda e, o=sqscr[:, 0:n], i=xA, acc=stA[:, 0:1]:
          e.activation(o, i, AF.Identity, bias=0.0, scale=1.0 / n, accum_out=acc),
          waits=wx)
        E("scalar", lambda e, o=sqscr[:, 0:n], i=xB, acc=stB[:, 0:1]:
          e.activation(o, i, AF.Identity, bias=0.0, scale=1.0 / n, accum_out=acc))
        E("scalar", lambda e, o=stA[:, 1:2], i=stA[:, 0:1]: e.mul(o, i, -1.0))
        E("scalar", lambda e, o=stB[:, 1:2], i=stB[:, 0:1]: e.mul(o, i, -1.0))
        E("scalar", lambda e, o=sqscr[:, 0:n], i=xA, bi=stA[:, 1:2], acc=stA[:, 2:3]:
          e.activation(o, i, AF.Square, bias=bi, scale=1.0, accum_out=acc))
        E("scalar", lambda e, o=sqscr[:, 0:n], i=xB, bi=stB[:, 1:2], acc=stB[:, 2:3]:
          e.activation(o, i, AF.Square, bias=bi, scale=1.0, accum_out=acc))
        E("scalar", lambda e, o=stA[:, 3:4], i=stA[:, 2:3]:
          e.activation(o, i, AF.Ln, bias=lnepsb, scale=1.0))
        E("scalar", lambda e, o=stB[:, 3:4], i=stB[:, 2:3]:
          e.activation(o, i, AF.Ln, bias=lnepsb, scale=1.0))
        E("scalar", lambda e, o=stA[:, 4:5], i=stA[:, 3:4]:
          e.activation(o, i, AF.Exp, bias=halflnb, scale=-0.5))
        E("scalar", lambda e, o=stB[:, 4:5], i=stB[:, 3:4]:
          e.activation(o, i, AF.Exp, bias=halflnb, scale=-0.5))
        E("scalar", lambda e, o=stA[:, 5:6], i=stA[:, 1:2], mm=stA[:, 4:5]: e.mul(o, i, mm))
        E("scalar", lambda e, o=stB[:, 5:6], i=stB[:, 1:2], mm=stB[:, 4:5]: e.mul(o, i, mm))
        E("scalar", lambda e, o=oA, i=xA, sc=stA[:, 4:5], bi=stA[:, 5:6]:
          e.activation(o, i, AF.Identity, bias=bi, scale=sc), incs=((sSC, 1),))
        E("scalar", lambda e, o=oB, i=xB, sc=stB[:, 4:5], bi=stB[:, 5:6]:
          e.activation(o, i, AF.Identity, bias=bi, scale=sc), incs=((sSC, 1),))
        ln_prev[0] = sSC.n
        return sSC.n

    def emit_transpose_cast(srctile, dest_fn, wait_sc, bankbase=6, extra_waits=()):
        for cp in range(4):
            bk = bankbase + (cp % 2)
            bank = ps[bk]
            for k in range(2):
                ct = 2 * cp + k
                E("tensor", lambda e, o=bank[:, k * 128:k * 128 + 128],
                  i=srctile[:, ct * 128:(ct + 1) * 128]:
                  e.matmul(o, i, identbf[:], start=True, stop=True),
                  waits=(((sSC, wait_sc), (sVE, identbf_ready[0])) + bank_wait(bk))
                        if k == 0 else (),
                  incs=(((sPE, 1),) if k == 1 else ()))
            pv = sPE.n
            E("vector", lambda e, d=dest_fn(cp),
              s=bank[:, 0:256].rearrange("p (c x) -> p c x", c=2):
              e.tensor_copy(d, s),
              waits=((sPE, pv),) + (tuple(extra_waits) if cp == 0 else ()),
              incs=((sVE, 1),))
            bank_consumed(bk, sVE, sVE.n)
        return sVE.n

    def XLN(ct):
        return AY[:, ct * 2048:(ct + 1) * 2048]

    def QT_(hp):
        return AQ[:, hp * 2048:(hp + 1) * 2048]

    def XLNp(cp, t2):
        return AY[:, 0:16384].rearrange("p (c x) -> p c x", c=8)[
            :, 2 * cp:2 * cp + 2, t2 * 128:(t2 + 1) * 128]

    def XLKp(cp, j):
        return xlk[:, 0:4096].rearrange("p (c x) -> p c x", c=8)[
            :, 2 * cp:2 * cp + 2, j * 128:(j + 1) * 128]

    def XL2p(cp, t2):
        return AQ[:, 0:16384].rearrange("p (c x) -> p c x", c=8)[
            :, 2 * cp:2 * cp + 2, t2 * 128:(t2 + 1) * 128]

    def P1():
        def emit_pair(tp):
            tA, tB = 2 * tp, 2 * tp + 1
            xb = xstg if tp % 2 == 0 else xstg2
            lb = lnstg if tp % 2 == 0 else lnstg2
            xbA = xb[:, 0:1024]
            xbB = xb[:, 1024:2048]
            lbA = lb[:, 0:1024]
            lbB = lb[:, 1024:2048]
            w = ((sVE, tr_done[tp - 2]),) if tp >= 2 else ()
            E("scalar", lambda e, d=xbA, s=x_own[tA * 128:(tA + 1) * 128, :]:
              e.dma_start(out=d, in_=s), waits=w, incs=((dX, 16),))
            E("scalar", lambda e, d=xbB, s=x_own[tB * 128:(tB + 1) * 128, :]:
              e.dma_start(out=d, in_=s), incs=((dX, 16),))
            vx = dX.n
            sc = emit_ln_pair(xbA, lbA, xbB, lbB, C, wait_x=((dX, vx),), war_wait=w)
            emit_transpose_cast(lbA, lambda cp, t2=tA: XLNp(cp, t2), sc)
            tr_done[tp] = emit_transpose_cast(
                lbB, lambda cp, t2=tB: XLNp(cp, t2), sc)

        def emit_qproj(ch):
            wlast = tr_done[2 * ch + 1]
            for hp in range(8):
                bk = (hp * 4 + ch) % 4
                bank = ps[bk]
                for ct in range(8):
                    E("tensor", lambda e, o=bank[:], l=WA(ct)[:, hp * 128:(hp + 1) * 128],
                      r=XLN(ct)[:, ch * 512:(ch + 1) * 512], st=(ct == 0), sp=(ct == 7):
                      e.matmul(o, l, r, start=st, stop=sp),
                      waits=(((sVE, wlast),) + bank_wait(bk)) if ct == 0 else (),
                      incs=(((sPE, 1),) if ct == 7 else ()))
                pv = sPE.n
                E("vector", lambda e, d=QT_(hp)[:, ch * 512:(ch + 1) * 512], s=bank[:],
                  bi=battn_sb[:, hp:hp + 1]:
                  e.tensor_scalar(d, s, bi, None, ALU.add),
                  waits=((sPE, pv),), incs=((sVE, 1),))
                bank_consumed(bk, sVE, sVE.n)
                qevac_done[hp * 4 + ch] = sVE.n

        for tp in range(NT // 2):
            emit_pair(tp)
            if tp >= 3 and tp % 2 == 1:
                emit_qproj((tp - 3) // 2)
        emit_qproj(3)

    tr_done = {}
    qevac_done = {}
    p2_mm_done = {}
    p2_pg0 = NT // 2
    p4_pg0 = NT // 2 + 32
    wp_done = {}
    spill_done = {}
    out_stg_free = {}

    def P1_dbg():
        if STAGE != 1:
            return
        E("scalar", None, waits=((sVE, qevac_done[31]),))
        for ct in range(8):
            E("scalar", lambda e, d=dbg["xlnT"][ct * 128:(ct + 1) * 128, :], s=XLN(ct):
              e.dma_start(out=d, in_=s), incs=((dX, 16),))
        for hp in range(8):
            E("scalar", lambda e, d=dbg["qT"][hp * 128:(hp + 1) * 128, :], s=QT_(hp):
              e.dma_start(out=d, in_=s), incs=((dX, 16),))
        E("scalar", lambda e, d=dbg["st"], s=small[:, 0:64]: e.dma_start(out=d, in_=s),
          incs=((dX, 16),))
        E("scalar", None, waits=((dX, dX.n),))
        E("scalar", lambda e, d=out_c[0:128, :], s=xstg[:, 0:1024]:
          e.dma_start(out=d, in_=s), incs=((dX, 16),))
        E("scalar", None, waits=((dX, dX.n),))

    def XLK(ct):
        return xlk[:, ct * 512:(ct + 1) * 512]

    def P2():
        kvev_free = {}
        vst_free = {}
        ln_sc = {}

        def emit_ln(cc):
            for pr in range(2):
                tA = 4 * cc + 2 * pr
                tB = tA + 1
                pg = p2_pg0 + 2 * cc + pr
                xb = xstg if pg % 2 == 0 else xstg2
                lb = lnstg if pg % 2 == 0 else lnstg2
                xbA = xb[:, 0:1024]
                xbB = xb[:, 1024:2048]
                lbA = lb[:, 0:1024]
                lbB = lb[:, 1024:2048]
                w = ((sVE, tr_done[pg - 2]),) if (pg - 2) in tr_done else ()
                E("scalar", lambda e, d=xbA, s=x_kv[tA * 128:(tA + 1) * 128, :]:
                  e.dma_start(out=d, in_=s), waits=w, incs=((dX, 16),))
                E("scalar", lambda e, d=xbB, s=x_kv[tB * 128:(tB + 1) * 128, :]:
                  e.dma_start(out=d, in_=s), incs=((dX, 16),))
                vx = dX.n
                ln_sc[(cc, pr)] = emit_ln_pair(xbA, lbA, xbB, lbB, C,
                                               wait_x=((dX, vx),), war_wait=w)

        def emit_tile_tr(cc, j):
            pr = j // 2
            pg = p2_pg0 + 2 * cc + pr
            lb = lnstg if pg % 2 == 0 else lnstg2
            lbt = lb[:, (j % 2) * 1024:(j % 2) * 1024 + 1024]
            v = emit_transpose_cast(
                lbt, lambda cp, j2=j: XLKp(cp, j2), ln_sc[(cc, pr)])
            if j % 2 == 1:
                tr_done[pg] = v

        def emit_K(cc):
            wx = tr_done[p2_pg0 + 2 * cc + 1]
            for kf in range(8):
                bk = kf % 4
                bank = ps[bk]
                for ct in range(8):
                    E("tensor", lambda e, o=bank[:], l=WA(ct)[:, 1024 + kf * 128:1024 + (kf + 1) * 128],
                      r=XLK(ct), st=(ct == 0), sp=(ct == 7):
                      e.matmul(o, l, r, start=st, stop=sp),
                      waits=(((sVE, wx),) + bank_wait(bk)) if ct == 0 else (),
                      incs=(((sPE, 1),) if ct == 7 else ()))
                pv = sPE.n
                kb = kf % 2
                kbuf = kvev[:, kb * 512:(kb + 1) * 512]
                E("vector", lambda e, d=kbuf, s=bank[:], bi=battn_sb[:, 8 + kf:9 + kf]:
                  e.tensor_scalar(d, s, bi, None, ALU.add),
                  waits=((sPE, pv), (dKV, kvev_free.get(kb, 0))), incs=((sVE, 1),))
                bank_consumed(bk, sVE, sVE.n)
                scv = sVE.n
                E("sync", lambda e, d=kslab[kf, :, cc * 512:(cc + 1) * 512], s=kbuf:
                  e.dma_start(out=d, in_=s), waits=((sVE, scv),), incs=((dKV, 16),))
                kvev_free[kb] = dKV.n

        def emit_V(cc, tj):
            bk = 4 + (tj % 2)
            bank = ps[bk]
            vb = tj % 2
            vbuf = vst[:, vb * 1040:(vb + 1) * 1040]
            for vc in range(2):
                for ct in range(8):
                    E("tensor", lambda e, o=bank[:], l=XLK(ct)[:, tj * 128:(tj + 1) * 128],
                      r=WA(ct)[:, 2048 + vc * 512:2048 + (vc + 1) * 512],
                      st=(ct == 0), sp=False:
                      e.matmul(o, l, r, start=st, stop=sp),
                      waits=(bank_wait(bk) if ct == 0 else ()),
                      incs=())
                E("tensor", lambda e, o=bank[:], l=rowbf[0:1, 1024:1152],
                  r=rowbf[0:1, vc * 512:(vc + 1) * 512]:
                  e.matmul(o, l, r, start=False, stop=True),
                  incs=((sPE, 1),))
                pv = sPE.n
                pv_s = bank[:].rearrange("p (hp hh d) -> p hp hh d", hp=4, hh=2)
                E("vector", lambda e, s=pv_s[:, :, 0, :],
                  d=vbuf.rearrange("p (hp y) -> p hp y", hp=8)[:, 4 * vc:4 * vc + 4, 0:64]:
                  e.tensor_copy(d, s),
                  waits=((sPE, pv), (dKV, vst_free.get(vb, 0))))
                E("vector", lambda e, s=pv_s[:, :, 1, :],
                  d=vbuf.rearrange("p (hp y) -> p hp y", hp=8)[:, 4 * vc:4 * vc + 4, 65:129]:
                  e.tensor_copy(d, s), incs=((sVE, 1),))
                bank_consumed(bk, sVE, sVE.n)
            vv = sVE.n
            E("sync", lambda e,
              d=vslab[:, cc * 512 + tj * 128: cc * 512 + (tj + 1) * 128, :]
                .rearrange("hp r y -> r hp y"),
              s=vbuf.rearrange("p (hp y) -> p hp y", hp=8):
              e.dma_start(out=d, in_=s), waits=((sVE, vv),), incs=((dKV, 16),))
            vst_free[vb] = dKV.n

        for cc in range(16):
            emit_ln(cc)
            if cc == 0:
                for j in range(4):
                    emit_tile_tr(0, j)
                continue
            emit_K(cc - 1)
            for tj in range(4):
                emit_V(cc - 1, tj)
                emit_tile_tr(cc, tj)
        emit_K(15)
        for tj in range(4):
            emit_V(15, tj)
        p2_mm_done[15] = sPE.n

    def P2_dbg():
        if STAGE != 2:
            return
        E("sync", None, waits=((dKV, dKV.n),))
        E("sync", lambda e, d=dbg["kslab"], s=kslab: e.dma_start(out=d, in_=s),
          incs=((dKV, 16),))
        E("sync", lambda e, d=dbg["vslab"], s=vslab: e.dma_start(out=d, in_=s),
          incs=((dKV, 16),))
        E("sync", None, waits=((dKV, dKV.n),))
        E("scalar", lambda e, d=out_c[0:128, :], s=xstg[:, 0:1024]:
          e.dma_start(out=d, in_=s), incs=((dX, 16),))
        E("scalar", None, waits=((dX, dX.n),))


    def P3():
        kvbuf = [(AK[:, 0:8192], AK[:, 8192:16512]),
                 (AW[:, 0:8192], AW[:, 8192:16512])]
        kv_ready = {}
        hp_mm_done = {}
        p2_all = dKV.n
        ybank = {(0, 0): 4, (0, 1): 5, (1, 0): 6, (1, 1): 7}
        pending = []
        norm_queue = []
        rw_set = [0, 0, 0, 0]
        estep = [0]

        def emit_av(avd):
            (hp2, par2, pvexp, mvs, avlist, kvb2) = avd
            for (h, yb, co, w2, tile, st_flag, sp_flag) in avlist:
                wts2 = [(sSC, pvexp)]
                if mvs is not None:
                    wts2.append((sVE, mvs))
                if st_flag:
                    wts2 += list(bank_wait(yb))
                E("tensor", lambda e, o=ps[yb][0:65, co:co + w2],
                  l=kvb2[:, tile * 130 + 65 * h: tile * 130 + 65 * h + 65],
                  r2=eT[:, par2 * 1024 + 512 * h: par2 * 1024 + 512 * h + w2],
                  st=st_flag, sp=sp_flag:
                  e.matmul(o, l, r2, start=st, stop=sp, skip_group_check=True),
                  waits=tuple(wts2), incs=((sPE, 1),))
                if sp_flag:
                    qc2 = {4: 0, 5: 1, 6: 0, 7: 1}[yb]
                    si = 2 * h + qc2
                    pvs2 = sPE.n
                    E("scalar", lambda e, o=oev[64:65, si * 512:(si + 1) * 512],
                      i=ps[yb][64:65, 0:512]:
                      e.activation(o, i, AF.Ln, bias=0.0, scale=1.0),
                      waits=((sPE, pvs2),))
                    E("scalar", lambda e, o=oev[64:65, si * 512:(si + 1) * 512],
                      i=oev[64:65, si * 512:(si + 1) * 512]: e.mul(o, i, -1.0))
                    E("scalar", lambda e, o=ofin[64:65, si * 512:(si + 1) * 512],
                      i=oev[64:65, si * 512:(si + 1) * 512]:
                      e.activation(o, i, AF.Exp, bias=0.0, scale=1.0), incs=((sSC, 1),))
                    rw_set[si] = sSC.n

        def flush_norm():
            while norm_queue:
                norm_queue.pop(0)()

        for hp in range(NHP):
            par = hp % 2
            kbuf, vbuf = kvbuf[par]
            bw = ((sPE, hp_mm_done.get(hp - 2, 0)),)
            E("sync", lambda e, d=kbuf, s=kslab[hp]: e.dma_start(out=d, in_=s),
              waits=((dKV, p2_all),) + bw, incs=((dKV, 16),))
            for tg in range(8):
                E("sync", lambda e,
                  d=vbuf.rearrange("p (tl y) -> p tl y", tl=64)[:, tg * 8:(tg + 1) * 8, :],
                  s=vslab[hp, tg * 1024:(tg + 1) * 1024, :].rearrange("(tl p) y -> p tl y", p=128):
                  e.dma_start(out=d, in_=s), incs=((dKV, 16),))
            kv_ready[hp] = dKV.n
            for qg in range(2):
                sched = []
                for a in range(8 * qg + 8):
                    su0 = max(qg * 1024, 128 * a)
                    su1 = qg * 1024 + 1024
                    subs = []
                    s0 = su0
                    nxt = 512 * (s0 // 512) + 512
                    while s0 < su1:
                        s1 = min(su1, nxt)
                        subs.append((s0, s1))
                        s0, nxt = s1, s1 + 512
                    for r in range(4):
                        for (s0b, s1b) in subs:
                            sched.append((a, r, s0b, s1b))
                last_touch = {}
                for idx, (a, r, s0b, s1b) in enumerate(sched):
                    qc = (s0b - qg * 1024) // 512
                    for h in range(2):
                        last_touch[(h, qc)] = idx
                started = set()
                for idx, (a, r, s0b, s1b) in enumerate(sched):
                    w = s1b - s0b
                    qc = (s0b - qg * 1024) // 512
                    co = s0b % 512
                    boundary = (a >= 8 * qg) and (s0b == max(qg * 1024, 128 * a))
                    pp = estep[0] % 2
                    estep[0] += 1
                    sD = psD[pp]
                    for h in range(2):
                        bkid = 2 * pp + h
                        E("tensor", lambda e, o=sD[:, 512 * h:512 * h + w],
                          l=kbuf[64 * h:64 * h + 64, a * 512 + r * 128:a * 512 + r * 128 + 128],
                          r2=QT_(hp)[64 * h:64 * h + 64, s0b:s1b], tp=(64 * h, 0):
                          e.matmul(o, l, r2, start=True, stop=True, tile_position=tp),
                          waits=((dKV, kv_ready[hp]),) + bank_wait(bkid),
                          incs=(((sPE, 1),) if h == 1 else ()))
                    pvs = sPE.n
                    eslot = eT[:, pp * 1024: pp * 1024 + 1024]
                    ein = sD[:].rearrange("p (h2 q) -> p h2 q", h2=2)[:, :, 0:w]
                    eout = eslot.rearrange("p (h2 q) -> p h2 q", h2=2)[:, :, 0:w]
                    E("scalar", lambda e, o=eout, i=ein:
                      e.activation(o, i, AF.Exp, bias=expbias, scale=0.125),
                      waits=((sPE, pvs),), incs=((sSC, 1),))
                    bank_consumed(2 * pp, sSC, sSC.n)
                    bank_consumed(2 * pp + 1, sSC, sSC.n)
                    pvexp = sSC.n
                    mvs = None
                    if boundary:
                        E("vector", lambda e, o=eslot[:, 0:128], m=cbf[:, r * 128:(r + 1) * 128]:
                          e.tensor_mul(o, o, m), waits=((sSC, pvexp),))
                        E("vector", lambda e, o=eslot[:, 512:640], m=cbf[:, r * 128:(r + 1) * 128]:
                          e.tensor_mul(o, o, m), incs=((sVE, 1),))
                        mvs = sVE.n
                    if idx == 1:
                        flush_norm()
                    if pending:
                        emit_av(pending.pop(0))
                    avlist = []
                    for h in range(2):
                        yb = ybank[(h, qc)]
                        st_flag = (h, qc) not in started
                        started.add((h, qc))
                        sp_flag = last_touch[(h, qc)] == idx
                        avlist.append((h, yb, co, w, a * 4 + r, st_flag, sp_flag))
                    pending.append((hp, pp, pvexp, mvs, avlist, vbuf))
                while pending:
                    emit_av(pending.pop(0))
                pv_all = sPE.n

                def make_norm(hp=hp, qg=qg, pv_all=pv_all, vbuf=vbuf,
                              rws=tuple(rw_set)):
                    def go():
                        sets = [(h, qc) for h in range(2) for qc in range(2)]
                        for si, (h, qc) in enumerate(sets):
                            yb = ybank[(h, qc)]
                            rbk = si % 2
                            E("tensor", lambda e, o=ps[rbk][0:64, 0:512],
                              l=small[64:65, 0:64], r2=ofin[64:65, si * 512:(si + 1) * 512]:
                              e.matmul(o, l, r2, start=True, stop=True),
                              waits=((sSC, rws[si]),) + bank_wait(rbk), incs=((sPE, 1),))
                            pv = sPE.n
                            E("vector", lambda e, o=t1s[0:64, 0:512], s=ps[rbk][0:64, 0:512]:
                              e.tensor_copy(o, s), waits=((sPE, pv),), incs=((sVE, 1),))
                            bank_consumed(rbk, sVE, sVE.n)
                            if h == 0:
                                dst = AY[0:64, hp * 2048 + qg * 1024 + qc * 512:
                                         hp * 2048 + qg * 1024 + qc * 512 + 512]
                            else:
                                dst = kvev[0:64, qc * 512:(qc + 1) * 512]
                            kw = ((dX, h2dma[0]),) if h == 1 else ()
                            E("vector", lambda e, o=dst, a2=ps[yb][0:64, 0:512],
                              b2=t1s[0:64, 0:512]:
                              e.tensor_mul(o, a2, b2), waits=kw, incs=((sVE, 1),))
                            bank_consumed(yb, sVE, sVE.n)
                        vv = sVE.n
                        E("scalar", lambda e,
                          o=AY[64:128, hp * 2048 + qg * 1024: hp * 2048 + qg * 1024 + 1024],
                          s=kvev[0:64, 0:1024]: e.dma_start(out=o, in_=s),
                          waits=((sVE, vv),), incs=((dX, 16),))
                        h2dma[0] = dX.n
                        h2_byqg[qg] = dX.n
                    return go
                norm_queue.append(make_norm())
            hp_mm_done[hp] = sPE.n
        flush_norm()
        E("tensor", None, waits=((dKV, dKV.n),))

    def P3_dbg():
        if STAGE != 3:
            return
        E("scalar", None, waits=((dKV, dKV.n), (dX, dX.n), (sVE, sVE.n)))
        for hp in range(8):
            E("scalar", lambda e, d=dbg["yT"][hp * 128:(hp + 1) * 128, :], s=AY[:, hp * 2048:(hp + 1) * 2048]:
              e.dma_start(out=d, in_=s), incs=((dX, 16),))
        E("scalar", lambda e, d=out_c[0:128, :], s=xstg[:, 0:1024]:
          e.dma_start(out=d, in_=s), incs=((dX, 16),))
        E("scalar", None, waits=((dX, dX.n),))

    def WPJ(ct):
        return AK[:, ct * 1024:(ct + 1) * 1024]

    def P4():
        for ft in range(8):
            buf = xstg[:, (ft % 2) * 1024:(ft % 2) * 1024 + 1024]
            wv = ((sPE, wp_done[ft - 2]),) if ft >= 2 else ()
            E("sync", lambda e, d=buf, s=W_proj[ft * 128:(ft + 1) * 128, :]:
              e.dma_start(out=d, in_=s), waits=wv, incs=((dKV, 16),))
            v = dKV.n
            for ct in range(8):
                bk = 6 + (ct % 2)
                bank = ps[bk]
                E("tensor", lambda e, o=bank[:, 0:128], i=buf[:, ct * 128:(ct + 1) * 128]:
                  e.transpose(o, i, identsb),
                  waits=((dKV, v),) + bank_wait(bk), incs=((sPE, 1),))
                pv = sPE.n
                E("vector", lambda e, d=WPJ(ct)[:, ft * 128:(ft + 1) * 128], s=bank[:, 0:128]:
                  e.tensor_copy(d, s), waits=((sPE, pv),), incs=((sVE, 1),))
                bank_consumed(bk, sVE, sVE.n)
            wp_done[ft] = sPE.n
        vfence()
        wpj_ready = sVE.n
        for ch in range(4):
            ev = {}
            for wave in range(2):
                stage_t = oev if wave == 0 else ofin
                for oti in range(4):
                    ot = 4 * wave + oti
                    bk = oti
                    for ct in range(8):
                        E("tensor", lambda e, o=ps[bk][:],
                          l=WPJ(ct)[:, ot * 128:(ot + 1) * 128],
                          r2=AY[:, ct * 2048 + ch * 512: ct * 2048 + (ch + 1) * 512],
                          st=(ct == 0), sp=(ct == 7):
                          e.matmul(o, l, r2, start=st, stop=sp),
                          waits=(((sVE, wpj_ready), (dX, h2_byqg[ch // 2])) + bank_wait(bk))
                                if ct == 0 else (),
                          incs=(((sPE, 1),) if ct == 7 else ()))
                    pv = sPE.n
                    E("scalar", lambda e, d=stage_t[:, oti * 512:(oti + 1) * 512],
                      s=ps[bk][:], bi=bproj_sb[:, ot:ot + 1]:
                      e.activation(d, s, AF.Identity, bias=bi, scale=1.0),
                      waits=((sPE, pv),), incs=((sSC, 1),))
                    bank_consumed(bk, sSC, sSC.n)
            sfence()
            ev_done2 = sSC.n
            for tt4 in range(4):
                tt = 4 * ch + tt4
                sl = tt4 % 2
                xb = xstg[:, sl * 1024:(sl + 1) * 1024]
                wv = ((dX, spill_done[tt - 2]),) if (tt - 2) in spill_done else ()
                E("scalar", lambda e, d=xb, s=x_own[tt * 128:(tt + 1) * 128, :]:
                  e.dma_start(out=d, in_=s), waits=wv, incs=((dX, 16),))
                vx = dX.n
                for ct in range(8):
                    stage_t = oev if ct < 4 else ofin
                    bk = 4 + ct // 4
                    E("tensor", lambda e, o=ps[bk][:, (ct % 4) * 128:(ct % 4) * 128 + 128],
                      i=stage_t[:, (ct % 4) * 512 + tt4 * 128:(ct % 4) * 512 + (tt4 + 1) * 128]:
                      e.transpose(o, i, identsb),
                      waits=((sSC, ev_done2),) + (bank_wait(bk) if ct % 4 == 0 else ()),
                      incs=((sPE, 1),))
                pv = sPE.n
                E("vector", lambda e, o=xb[:, 0:512], a2=ps[4][:], b2=xb[:, 0:512]:
                  e.tensor_add(o, a2, b2), waits=((sPE, pv), (dX, vx)))
                E("vector", lambda e, o=xb[:, 512:1024], a2=ps[5][:], b2=xb[:, 512:1024]:
                  e.tensor_add(o, a2, b2), incs=((sVE, 1),))
                bank_consumed(4, sVE, sVE.n)
                bank_consumed(5, sVE, sVE.n)
                av = sVE.n
                E("scalar", lambda e, d=x2sp[tt * 128:(tt + 1) * 128, :], s=xb:
                  e.dma_start(out=d, in_=s), waits=((sVE, av),), incs=((dX, 16),))
                spill_done[tt] = dX.n
        for tp in range(NT // 2):
            tA, tB = 2 * tp, 2 * tp + 1
            pg = p4_pg0 + tp
            xb = xstg if tp % 2 == 0 else xstg2
            lb = lnstg if tp % 2 == 0 else lnstg2
            xbA, xbB = xb[:, 0:1024], xb[:, 1024:2048]
            lbA, lbB = lb[:, 0:1024], lb[:, 1024:2048]
            w = ((sVE, tr_done[pg - 2]),) if tp >= 2 else \
                ((dX, spill_done[15]),)
            E("scalar", lambda e, d=xbA, s=x2sp[tA * 128:(tA + 1) * 128, :]:
              e.dma_start(out=d, in_=s), waits=w, incs=((dX, 16),))
            E("scalar", lambda e, d=xbB, s=x2sp[tB * 128:(tB + 1) * 128, :]:
              e.dma_start(out=d, in_=s), incs=((dX, 16),))
            vx = dX.n
            sc = emit_ln_pair(xbA, lbA, xbB, lbB, C, wait_x=((dX, vx),), war_wait=w)
            emit_transpose_cast(lbA, lambda cp, t2=tA: XL2p(cp, t2), sc)
            tr_done[pg] = emit_transpose_cast(
                lbB, lambda cp, t2=tB: XL2p(cp, t2), sc)

    def XL2(ct):
        return AQ[:, ct * 2048:(ct + 1) * 2048]

    def P4_dbg():
        if STAGE != 4:
            return
        E("scalar", None, waits=((sVE, tr_done[p4_pg0 + 7]), (dX, dX.n)))
        E("scalar", lambda e, d=dbg["x2"], s=x2sp: e.dma_start(out=d, in_=s),
          incs=((dX, 16),))
        for ct in range(8):
            E("scalar", lambda e, d=dbg["xln2T"][ct * 128:(ct + 1) * 128, :], s=XL2(ct):
              e.dma_start(out=d, in_=s), incs=((dX, 16),))
        E("scalar", lambda e, d=out_c[0:128, :], s=xstg[:, 0:1024]:
          e.dma_start(out=d, in_=s), incs=((dX, 16),))
        E("scalar", None, waits=((dX, dX.n),))

    def P5():
        gw = kvev[:, 0:128].rearrange("p (ct e2) -> p ct e2", ct=8)
        E("gpsimd", lambda e, d=gw, s=gate_w.rearrange("(ct p) e2 -> p ct e2", p=128):
          e.dma_start(out=d, in_=s),
          waits=((dKV, dKV.n), (sVE, sVE.n)), incs=((dW, 16),))
        gwv = dW.n
        for ch in range(4):
            bk = ch % 2
            for ct in range(8):
                E("tensor", lambda e, o=ps[bk][0:16, 0:512], l=gw[:, ct, :],
                  r2=XL2(ct)[:, ch * 512:(ch + 1) * 512], st=(ct == 0), sp=(ct == 7):
                  e.matmul(o, l, r2, start=st, stop=sp),
                  waits=(((dW, gwv), (sVE, tr_done[p4_pg0 + 7])) + bank_wait(bk))
                        if ct == 0 else (),
                  incs=(((sPE, 1),) if ct == 7 else ()))
                pv = sPE.n
            E("scalar", lambda e, d=rT16[0:16, ch * 512:(ch + 1) * 512],
              s=ps[bk][0:16, 0:512], bi=gateb_sb[0:16, 0:1]:
              e.activation(d, s, AF.Sigmoid, bias=bi, scale=1.0),
              waits=((sPE, pv),), incs=((sSC, 1),))
            bank_consumed(bk, sSC, sSC.n)
        sfence()
        rfv = sSC.n
        for tg in range(4):
            tls = [4 * tg + i for i in range(4)]
            for i, t in enumerate(tls):
                bk = 2 + i % 2
                E("tensor", lambda e, o=ps[bk][0:128, 0:16],
                  i2=rT16[0:16, t * 128:(t + 1) * 128], idn=identsb[0:16, 0:16]:
                  e.transpose(o, i2, idn),
                  waits=((sSC, rfv),) + bank_wait(bk), incs=((sPE, 1),))
                pv = sPE.n
                E("vector", lambda e, o=rtok[:, 16 * i:16 * i + 16], s=ps[bk][0:128, 0:16]:
                  e.tensor_copy(o, s), waits=((sPE, pv),), incs=((sVE, 1),))
                bank_consumed(bk, sVE, sVE.n)
            for i in range(4):
                E("vector", lambda e, o=small[:, 64 + i:65 + i], s=rtok[:, 16 * i:16 * i + 16]:
                  e.reduce_max(o, s, axis=AX))
            for i in range(4):
                E("vector", lambda e, o=wsl[:, 16 * i:16 * i + 16],
                  s=rtok[:, 16 * i:16 * i + 16], m=small[:, 64 + i:65 + i]:
                  e.tensor_scalar(o, s, m, None, ALU.is_equal))
            for i in range(4):
                E("vector", lambda e, o=r2sl[:, 16 * i:16 * i + 16],
                  a2=wsl[:, 16 * i:16 * i + 16], s=rtok[:, 16 * i:16 * i + 16]:
                  e.scalar_tensor_tensor(o, a2, -1e30, s, ALU.mult, ALU.add))
            for i in range(4):
                E("vector", lambda e, o=small[:, 68 + i:69 + i], s=r2sl[:, 16 * i:16 * i + 16]:
                  e.reduce_max(o, s, axis=AX))
            for i in range(4):
                E("vector", lambda e, o=r2sl[:, 16 * i:16 * i + 16],
                  s=r2sl[:, 16 * i:16 * i + 16], m=small[:, 68 + i:69 + i]:
                  e.tensor_scalar(o, s, m, None, ALU.is_equal))
            for i in range(4):
                E("vector", lambda e, o=wsl[:, 16 * i:16 * i + 16],
                  a2=wsl[:, 16 * i:16 * i + 16], b2=r2sl[:, 16 * i:16 * i + 16]:
                  e.tensor_add(o, a2, b2))
            for i in range(4):
                E("vector", lambda e, o=small[:, 72 + i:73 + i],
                  a2=small[:, 64 + i:65 + i], b2=small[:, 68 + i:69 + i]:
                  e.tensor_add(o, a2, b2))
            vfence()
            vv = sVE.n
            for i in range(4):
                E("scalar", lambda e, o=small[:, 76 + i:77 + i], s=small[:, 72 + i:73 + i]:
                  e.activation(o, s, AF.Ln, bias=0.0, scale=1.0), waits=((sVE, vv),))
            for i in range(4):
                E("scalar", lambda e, o=small[:, 76 + i:77 + i], s=small[:, 76 + i:77 + i]:
                  e.mul(o, s, -1.0))
            for i in range(4):
                E("scalar", lambda e, o=small[:, 80 + i:81 + i], s=small[:, 76 + i:77 + i]:
                  e.activation(o, s, AF.Exp, bias=0.0, scale=1.0), incs=((sSC, 1),))
            sfence()
            sv = sSC.n
            for i in range(4):
                E("vector", lambda e, o=wsl[:, 16 * i:16 * i + 16],
                  a2=wsl[:, 16 * i:16 * i + 16], b2=rtok[:, 16 * i:16 * i + 16]:
                  e.tensor_mul(o, a2, b2), waits=((sSC, sv),))
            for i in range(4):
                E("vector", lambda e, o=wsl[:, 16 * i:16 * i + 16],
                  s=wsl[:, 16 * i:16 * i + 16], m=small[:, 80 + i:81 + i]:
                  e.tensor_scalar(o, s, m, None, ALU.mult), incs=((sVE, 1),))
            vfence()
            wv2 = sVE.n
            for i, t in enumerate(tls):
                bk = 2 + i % 2
                E("tensor", lambda e, o=ps[bk][0:16, 0:128],
                  i2=wsl[:, 16 * i:16 * i + 16], idn=identsb:
                  e.transpose(o, i2, idn),
                  waits=((sVE, wv2),) + bank_wait(bk), incs=((sPE, 1),))
                pv = sPE.n
                E("vector", lambda e, o=wT16[0:16, t * 128:(t + 1) * 128], s=ps[bk][0:16, 0:128]:
                  e.tensor_copy(o, s), waits=((sPE, pv),), incs=((sVE, 1),))
                bank_consumed(bk, sVE, sVE.n)
        vfence()

    def P5_dbg():
        if STAGE != 5:
            return
        E("scalar", None, waits=((sVE, sVE.n),))
        E("scalar", lambda e, d=dbg["wT"], s=wT16[0:16, :]: e.dma_start(out=d, in_=s),
          incs=((dX, 16),))
        E("scalar", lambda e, d=dbg["rT"], s=rT16[0:16, :]: e.dma_start(out=d, in_=s),
          incs=((dX, 16),))
        E("scalar", lambda e, d=dbg["rtok"], s=rtok[:, 0:64]: e.dma_start(out=d, in_=s),
          incs=((dX, 16),))
        E("scalar", lambda e, d=out_c[0:128, :], s=xstg[:, 0:1024]:
          e.dma_start(out=d, in_=s), incs=((dX, 16),))
        E("scalar", None, waits=((dX, dX.n),))

    def P6():
        E("gpsimd", lambda e, d=eT[0:16, 0:2048], s=selb: e.dma_start(out=d, in_=s),
          waits=((sPE, sPE.n),), incs=((dW, 16),))
        selv = dW.n
        comb_guard = {}
        out_done = {}
        wload = {}
        wfree = {-2: sPE.n, -1: sPE.n}
        hp_free = {}
        last_t1 = [0]
        ffw_free = {}
        acc_add = {}
        stg_free = {}

        def emit_combine(ch):
            c0 = ch * 256
            slot = ch % 2
            stg = oev if slot == 0 else ofin
            sw = comb_guard.get(slot, ())
            E("sync", lambda e,
              d=stg[:, 0:2048].rearrange("p (ot t2) -> p ot t2", ot=8),
              s=ffacc.rearrange("(ot p) t2 -> p ot t2", p=128)[:, :, c0:c0 + 256]:
              e.dma_start(out=d, in_=s),
              waits=((dW, acc_add[(16, ch)]),) + sw, incs=((dKV, 16),))
            lv = dKV.n
            for ot in range(8):
                E("vector", lambda e, o=stg[:, ot * 256:(ot + 1) * 256],
                  a2=stg[:, ot * 256:(ot + 1) * 256], b2=XL2(ot)[:, c0:c0 + 256]:
                  e.tensor_add(o, a2, b2),
                  waits=((dKV, lv),), incs=(((sVE, 1),) if ot == 7 else ()))
            vfence()
            adv = sVE.n
            for tt2 in range(2):
                tt = 2 * ch + tt2
                sl = tt2
                xb = xstg[:, sl * 1024:(sl + 1) * 1024]
                wv2 = ((dX, out_done[tt - 2]),) if (tt - 2) in out_done else ()
                E("scalar", lambda e, d=xb, s=x2sp[tt * 128:(tt + 1) * 128, :]:
                  e.dma_start(out=d, in_=s), waits=wv2, incs=((dX, 16),))
                vx = dX.n
                for ct in range(8):
                    bk = 4 + ct // 4
                    E("tensor", lambda e, o=ps[bk][:, (ct % 4) * 128:(ct % 4) * 128 + 128],
                      i=stg[:, ct * 256 + tt2 * 128: ct * 256 + (tt2 + 1) * 128]:
                      e.transpose(o, i, identsb),
                      waits=((sVE, adv),) + (bank_wait(bk) if ct % 4 == 0 else ()),
                      incs=((sPE, 1),))
                pv = sPE.n
                fb = hprime[:, 0:1024].rearrange("p x -> p x") if False else None
                E("vector", lambda e, o=xb[:, 0:512], a2=ps[4][:], b2=xb[:, 0:512]:
                  e.tensor_add(o, a2, b2), waits=((sPE, pv), (dX, vx)))
                E("vector", lambda e, o=xb[:, 512:1024], a2=ps[5][:], b2=xb[:, 512:1024]:
                  e.tensor_add(o, a2, b2), incs=((sVE, 1),))
                bank_consumed(4, sVE, sVE.n)
                bank_consumed(5, sVE, sVE.n)
                fv = sVE.n
                E("scalar", lambda e, d=out_c[tt * 128:(tt + 1) * 128, :], s=xb:
                  e.dma_start(out=d, in_=s), waits=((sVE, fv),), incs=((dX, 16),))
                out_done[tt] = dX.n
            comb_guard[slot] = ((sPE, pv),)

        for e_ in range(17):
            par = e_ % 2
            base = par * 12288
            wv = ((sPE, wfree.get(e_ - 2, 0)), (dW, cast_dw[16]))
            for pi in range(3):
                off = (0, 4096, 8192)[pi]
                E("sync", lambda e, d=AW[:, base + off:base + off + 4096], s=wslab[e_, pi]:
                  e.dma_start(out=d, in_=s),
                  waits=wv if pi == 0 else (), incs=((dKV, 16),))
            wload[e_] = dKV.n
            if e_ > 0:
                fsl = ffw[:, (e_ % 2) * 2048:(e_ % 2) * 2048 + 2048]
                for ch in range(8):
                    bk = 4 + ch % 2
                    E("tensor", lambda e, o=ps[bk][0:128, 0:256],
                      l=eT[0:16, (e_ - 1) * 128:e_ * 128],
                      r2=wT16[0:16, ch * 256:(ch + 1) * 256]:
                      e.matmul(o, l, r2, start=True, stop=True),
                      waits=((dW, selv), (sVE, ffw_free.get(e_ - 2, 0))) + bank_wait(bk),
                      incs=((sPE, 1),))
                    pv = sPE.n
                    E("vector", lambda e, o=fsl[:, ch * 256:(ch + 1) * 256], s=ps[bk][0:128, 0:256]:
                      e.tensor_copy(o, s), waits=((sPE, pv),), incs=((sVE, 1),))
                    bank_consumed(bk, sVE, sVE.n)
                vfence()
                ffw_free[e_] = sVE.n
            for ch in range(8):
                c0 = ch * 256
                pvgu = {}
                for hf in range(2):
                    gb, ub = (4, 5) if hf == 0 else (6, 7)
                    for mt in range(2):
                        for ct in range(8):
                            E("tensor", lambda e, o=ps[gb][:, mt * 256:(mt + 1) * 256],
                              l=AW[:, base + ct * 512 + hf * 256 + mt * 128:
                                    base + ct * 512 + hf * 256 + mt * 128 + 128],
                              r2=XL2(ct)[:, c0:c0 + 256], st=(ct == 0), sp=(ct == 7):
                              e.matmul(o, l, r2, start=st, stop=sp),
                              waits=(((dKV, wload[e_]),) + bank_wait(gb)) if (mt == 0 and ct == 0) else (),
                              incs=())
                    for mt in range(2):
                        for ct in range(8):
                            E("tensor", lambda e, o=ps[ub][:, mt * 256:(mt + 1) * 256],
                              l=AW[:, base + 4096 + ct * 512 + hf * 256 + mt * 128:
                                    base + 4096 + ct * 512 + hf * 256 + mt * 128 + 128],
                              r2=XL2(ct)[:, c0:c0 + 256], st=(ct == 0), sp=(ct == 7):
                              e.matmul(o, l, r2, start=st, stop=sp),
                              waits=(bank_wait(ub)) if (mt == 0 and ct == 0) else (),
                              incs=(((sPE, 1),) if (mt == 1 and ct == 7) else ()))
                    pvgu[hf] = sPE.n
                for hf in range(2):
                    gb, ub = (4, 5) if hf == 0 else (6, 7)
                    pv = pvgu[hf]
                    hw = ((sPE, hp_free.get((e_, ch - 1, hf), hp_free.get((e_ - 1, 7, hf), 0))),)
                    E("scalar", lambda e, o=t1s[:, 0:512], s=ps[gb][:]:
                      e.activation(o, s, AF.Silu, bias=0.0, scale=1.0),
                      waits=((sPE, pv), (sVE, last_t1[0])) + hw, incs=((sSC, 1),))
                    scv = sSC.n
                    hslot = hprime[:, hf * 512:(hf + 1) * 512]
                    E("vector", lambda e, o=hslot, a2=t1s[:, 0:512], b2=ps[ub][:]:
                      e.tensor_mul(o, a2, b2), waits=((sSC, scv),) + hw)
                    bank_consumed(gb, sVE, sVE.n + 1)
                    bank_consumed(ub, sVE, sVE.n + 1)
                    if e_ > 0:
                        fsl = ffw[:, (e_ % 2) * 2048:(e_ % 2) * 2048 + 2048]
                        E("vector", lambda e, o=hslot[:, 0:256], a2=hslot[:, 0:256],
                          b2=fsl[:, c0:c0 + 256]:
                          e.tensor_mul(o, a2, b2), waits=((sVE, ffw_free[e_]),))
                        E("vector", lambda e, o=hslot[:, 256:512], a2=hslot[:, 256:512],
                          b2=fsl[:, c0:c0 + 256]:
                          e.tensor_mul(o, a2, b2), incs=((sVE, 1),))
                    else:
                        E("vector", lambda e, o=hslot[:, 0:1], a2=hslot[:, 0:1],
                          b2=hslot[:, 0:1]: e.tensor_copy(o, a2), incs=((sVE, 1),))
                    vfence()
                    hv = sVE.n
                    last_t1[0] = hv
                    for ct2 in range(2):
                        for ot in range(8):
                            st_flag = (hf == 0 and ct2 == 0)
                            sp_flag = (hf == 1 and ct2 == 1)
                            obk = ps[ot // 2][:, (ot % 2) * 256:(ot % 2) * 256 + 256]
                            E("tensor", lambda e, o=obk,
                              l=AW[:, base + 8192 + (2 * hf + ct2) * 1024 + ot * 128:
                                    base + 8192 + (2 * hf + ct2) * 1024 + ot * 128 + 128],
                              r2=hslot[:, ct2 * 256:(ct2 + 1) * 256],
                              st=(st_flag and ot % 2 == 0), sp=sp_flag:
                              e.matmul(o, l, r2, start=st, stop=sp, skip_group_check=True),
                              waits=((sVE, hv),) + (bank_wait(ot // 2) if (hf == 0 and ct2 == 0 and ot % 2 == 0) else ()),
                              incs=((sPE, 1),) if (ct2 == 1 and ot == 7) else ())
                    hp_free[(e_, ch, hf)] = sPE.n
                pv2 = sPE.n
                if e_ == 16 and ch >= 2:
                    emit_combine(ch - 2)
                slot = (e_ * 8 + ch) % 2
                stg = oev if slot == 0 else ofin
                sw = ((dW, stg_free.get(slot, 0)),) + comb_guard.get(slot, ())
                for bb in range(4):
                    E("vector", lambda e, o=stg[:, bb * 512:(bb + 1) * 512], s=ps[bb][:]:
                      e.tensor_copy(o, s),
                      waits=((sPE, pv2),) + sw, incs=(((sVE, 1),) if bb == 3 else ()))
                    bank_consumed(bb, sVE, sVE.n + (1 if bb < 3 else 0))
                vfence()
                ev = sVE.n
                aw2 = ((dW, acc_add.get((e_ - 1, ch), 0)),)
                E("gpsimd", lambda e,
                  d=ffacc.rearrange("(ot p) t2 -> p ot t2", p=128)[:, :, c0:c0 + 256],
                  s=stg[:, 0:2048].rearrange("p (ot t2) -> p ot t2", ot=8),
                  op=(ALU.add if e_ > 0 else ALU.bypass):
                  e.dma_start(out=d, in_=s, accum_op=op),
                  waits=((sVE, ev),) + aw2, incs=((dW, 16),))
                acc_add[(e_, ch)] = dW.n
                stg_free[slot] = dW.n
            wfree[e_] = sPE.n
        for ch in range(6, 8):
            emit_combine(ch)
        E("scalar", None, waits=((dX, dX.n),))

    P0()
    P0_wattn()
    P1()
    P1_dbg()
    if STAGE >= 2:
        P2()
        P0c()
        P2_dbg()
    if STAGE >= 3:
        P3()
        P3_dbg()
    if STAGE >= 4:
        P4()
        P4_dbg()
    if STAGE >= 5:
        P5()
        P5_dbg()
    if STAGE >= 6:
        P6()

    block = es.enter_context(nc.Block())
    with es:
        p.finalize(block)
    return nc


def make_inputs(inputs):
    x = np.asarray(inputs["L0_attn_x"], dtype=np.float32)
    masks_all = []
    in_maps = []
    shared = dict(
        W_attn=np.asarray(inputs["L0_attn_W_attn"], np.float32),
        b_attn=np.asarray(inputs["L0_attn_b_attn"], np.float32),
        W_proj=np.asarray(inputs["L0_attn_W_proj"], np.float32),
        b_proj=np.asarray(inputs["L0_attn_b_proj"], np.float32),
        gate_w=np.asarray(inputs["L0_ffn_gate_w"], np.float32),
        gate_b=np.asarray(inputs["L0_ffn_gate_b"], np.float32),
        sh_wg=np.asarray(inputs["L0_ffn_shared_wg"], np.float32),
        sh_wu=np.asarray(inputs["L0_ffn_shared_wu"], np.float32),
        sh_wd=np.asarray(inputs["L0_ffn_shared_wd"], np.float32),
        ex_wg=np.asarray(inputs["L0_ffn_experts_wg"], np.float32),
        ex_wu=np.asarray(inputs["L0_ffn_experts_wu"], np.float32),
        ex_wd=np.asarray(inputs["L0_ffn_experts_wd"], np.float32),
        identf=np.eye(128, dtype=np.float32),
    )
    for c in range(8):
        b, m = divmod(c, 4)
        o = 384 - 128 * m
        x_own = np.concatenate([x[b, 512 * t + o: 512 * t + o + 128] for t in range(NT)])
        kk = np.arange(128)[:, None]
        qq = np.arange(128)[None, :]
        mk = np.stack([( (128 * r + kk) <= (o + qq)).astype(BF) for r in range(4)])
        im = dict(shared)
        im["x_own"] = np.ascontiguousarray(x_own)
        im["selb"] = np.kron(np.eye(16, dtype=np.float32), np.ones((1, 128), np.float32)).astype(BF)
        im["x_kv"] = np.ascontiguousarray(x[b])
        im["masks"] = mk
        in_maps.append(im)
    return in_maps


def kernel(**inputs):
    in_maps = make_inputs(inputs)
    nc = build_nc()
    res = run_bass_kernel_spmd(nc, in_maps, core_ids=list(range(8)))
    x = np.asarray(inputs["L0_attn_x"], np.float32)
    out = np.empty_like(x)
    for c in range(8):
        b, m = divmod(c, 4)
        o = 384 - 128 * m
        oc = res.results[c]["out_c"]
        for t in range(NT):
            out[b, 512 * t + o: 512 * t + o + 128] = oc[128 * t: 128 * (t + 1)]
    return out


def debug_run(inputs):
    in_maps = make_inputs(inputs)
    nc = build_nc()
    res = run_bass_kernel_spmd(nc, in_maps, core_ids=list(range(8)))
    return res, in_maps


# version 17
# speedup vs baseline: 1.1671x; 1.0028x over previous
import contextlib
import os
import numpy as np
import ml_dtypes
import concourse.bass as bass
import concourse.mybir as mybir
from concourse.bass_utils import run_bass_kernel_spmd

F32 = mybir.dt.float32
BF16 = mybir.dt.bfloat16
BF = ml_dtypes.bfloat16
AF = mybir.ActivationFunctionType
ALU = mybir.AluOpType
AX = mybir.AxisListType.X

B, T, C, H, HD = 2, 8192, 1024, 16, 64
NT, QT = 16, 128
TOK = NT * QT
NHP = H // 2
NE, I_FF = 16, 512
EXP_BIAS = -12.0
EPS = 1e-5
STAGE = 6

ENGS = ("sync", "scalar", "vector", "tensor", "gpsimd")


class Sem:
    def __init__(self, nc, es, name):
        self.h = es.enter_context(nc.semaphore(name))
        self.n = 0
        self.name = name

    def bump(self, k):
        self.n += k
        return self.n


class Prog:
    def __init__(self, nc, es):
        self.nc = nc
        self.es = es
        self.stream = {e: [] for e in ENGS}
        self.hw = {e: {} for e in ENGS}

    def sem(self, name):
        return Sem(self.nc, self.es, name)

    def emit(self, eng, builder, waits=(), incs=(), dma_selfwait=True):
        waits = tuple(waits)
        for s, k in incs:
            if k == 16 and dma_selfwait:
                waits = ((s, s.n),) + waits
        wl = []
        for s, v in waits:
            if self.hw[eng].get(s.name, -1) < v:
                self.hw[eng][s.name] = v
                wl.append((s.h, v))
        il = []
        for s, k in incs:
            s.bump(k)
            il.append((s.h, k, s.n))
        self.stream[eng].append((builder, wl, il))

    def wait(self, eng, sem, val):
        self.emit(eng, None, waits=((sem, val),))

    def finalize(self, block):
        def mk(eng_name):
            def run(engine):
                for builder, wl, il in self.stream[eng_name]:
                    for h, v in wl:
                        engine.wait_ge(h, v)
                    if builder is None:
                        continue
                    inst = builder(engine)
                    for h, k, _ in il:
                        inst.then_inc(h, k)
            return run
        block.sync(mk("sync"))
        block.scalar(mk("scalar"))
        block.vector(mk("vector"))
        block.tensor(mk("tensor"))
        block.gpsimd(mk("gpsimd"))


def build_nc():
    nc = bass.Bass("TRN2", target_bir_lowering=False, detect_race_conditions=False)
    es = contextlib.ExitStack()
    p = Prog(nc, es)

    def din(name, shape, dt=F32):
        return nc.dram_tensor(name, shape, dt, kind="ExternalInput").ap()

    x_own = din("x_own", [TOK, C])
    x_kv = din("x_kv", [T, C])
    masks = din("masks", [4, 128, 128], BF16)
    identf = din("identf", [128, 128], F32)
    W_attn = din("W_attn", [3 * C, C])
    b_attn = din("b_attn", [3 * C])
    W_proj = din("W_proj", [C, C])
    b_proj = din("b_proj", [C])
    gate_w = din("gate_w", [C, NE])
    gate_b = din("gate_b", [NE])
    sh_wg = din("sh_wg", [C, I_FF])
    sh_wu = din("sh_wu", [C, I_FF])
    sh_wd = din("sh_wd", [I_FF, C])
    ex_wg = din("ex_wg", [NE, C, I_FF])
    ex_wu = din("ex_wu", [NE, C, I_FF])
    ex_wd = din("ex_wd", [NE, I_FF, C])
    selb = din("selb", [16, 2048], BF16)

    out_c = nc.dram_tensor("out_c", [TOK, C], F32, kind="ExternalOutput").ap()
    dbg = {}
    if STAGE == 1:
        dbg["qT"] = nc.dram_tensor("dbg_qT", [C, TOK], BF16, kind="ExternalOutput").ap()
        dbg["xlnT"] = nc.dram_tensor("dbg_xlnT", [C, TOK], BF16, kind="ExternalOutput").ap()
        dbg["st"] = nc.dram_tensor("dbg_st", [128, 64], F32, kind="ExternalOutput").ap()
    if STAGE == 2:
        dbg["kslab"] = nc.dram_tensor("dbg_kslab", [NHP, 128, T], BF16, kind="ExternalOutput").ap()
        dbg["vslab"] = nc.dram_tensor("dbg_vslab", [NHP, T, 130], BF16, kind="ExternalOutput").ap()
    if STAGE == 3:
        dbg["yT"] = nc.dram_tensor("dbg_yT", [C, TOK], BF16, kind="ExternalOutput").ap()
    if STAGE == 4:
        dbg["x2"] = nc.dram_tensor("dbg_x2", [TOK, C], F32, kind="ExternalOutput").ap()
        dbg["xln2T"] = nc.dram_tensor("dbg_xln2T", [C, TOK], BF16, kind="ExternalOutput").ap()
    if False:
        dbg["acc0"] = nc.dram_tensor("dbg_acc0", [128, 2048], F32, kind="ExternalOutput").ap()
        dbg["acc1"] = nc.dram_tensor("dbg_acc1", [128, 2048], F32, kind="ExternalOutput").ap()
        dbg["ffw"] = nc.dram_tensor("dbg_ffw", [128, 4096], BF16, kind="ExternalOutput").ap()
        dbg["hp0"] = nc.dram_tensor("dbg_hp0", [128, 1024], BF16, kind="ExternalOutput").ap()
        dbg["oev"] = nc.dram_tensor("dbg_oev", [128, 2048], F32, kind="ExternalOutput").ap()
    if STAGE == 5:
        dbg["wT"] = nc.dram_tensor("dbg_wT", [NE, TOK], BF16, kind="ExternalOutput").ap()
        dbg["rT"] = nc.dram_tensor("dbg_rT", [NE, TOK], F32, kind="ExternalOutput").ap()
        dbg["rtok"] = nc.dram_tensor("dbg_rtok", [128, 64], F32, kind="ExternalOutput").ap()

    kslab = nc.dram_tensor("kslab", [NHP, 128, T], BF16, kind="Internal").ap()
    vslab = nc.dram_tensor("vslab", [NHP, T, 130], BF16, kind="Internal").ap()
    x2sp = nc.dram_tensor("x2sp", [TOK, C], F32, kind="Internal").ap()
    wslab = nc.dram_tensor("wslab", [17, 3, 128, 4096], BF16, kind="Internal").ap()
    ffacc = nc.dram_tensor("ffacc", [C, TOK], F32, kind="Internal").ap()

    def sb(name, shape, dt):
        return es.enter_context(nc.sbuf_tensor(name, shape, dt))

    AW = sb("AW", [128, 24576], BF16)
    AQ = sb("AQ", [128, 16384], BF16)
    AY = sb("AY", [128, 16384], BF16)
    AK = sb("AK", [128, 16512], BF16)
    xstg = sb("xstg", [128, 2048], F32)
    lnstg = sb("lnstg", [128, 2048], BF16)
    xstg2 = sb("xstg2", [128, 2048], F32)
    lnstg2 = sb("lnstg2", [128, 2048], BF16)
    identbf = sb("identbf", [128, 128], BF16)
    stats = sb("stats", [128, 16], F32)
    xlk = sb("xlk", [128, 4096], BF16)
    kvev = sb("kvev", [128, 1024], BF16)
    eT = sb("eT", [128, 2048], BF16)
    vst = sb("vst", [128, 2080], BF16)
    consts = sb("consts", [128, 136], F32)
    cbf = sb("cbf", [128, 512], BF16)
    rowbf = sb("rowbf", [1, 1152], BF16)
    rowf = sb("rowf", [1, 128], F32)
    small = sb("small", [128, 128], F32)
    hprime = sb("hprime", [128, 1024], BF16)
    t1s = sb("t1s", [128, 512], F32)
    oev = sb("oev", [128, 2048], F32)
    rtok = sb("rtok", [128, 64], F32)
    wsl = sb("wsl", [128, 64], F32)
    r2sl = sb("r2sl", [128, 64], F32)
    wstg = oev
    sqscr = eT[:, 0:1024]
    ffw = nc.virtual_sbuf_tensor("ffw", [128, 4096], BF16)
    nc.back_virtual_tensor_with(ffw, xlk)
    wT16 = vst[0:16, 0:2048]
    ofin = xstg2
    rT16 = oev[0:16, :]

    battn_sb = sb("battn", [128, 24], F32)
    bproj_sb = sb("bproj", [128, 8], F32)
    gateb_sb = sb("gateb", [16, 1], F32)

    identsb = consts[:, 0:128]
    expbias = consts[:, 128:129]
    lnepsb = consts[:, 129:130]
    halflnb = consts[:, 130:131]

    psD = [es.enter_context(nc.psum_tensor(f"psD{i}", [128, 1024], F32)) for i in range(4)]
    ps = [psD[i // 2][:, (i % 2) * 512:(i % 2) * 512 + 512] for i in range(8)]
    psg = {}

    def bank_wait(i):
        g = psg.get(i)
        return (g,) if g else ()

    def bank_consumed(i, sem, val):
        psg[i] = (sem, val)

    def vfence():
        E("vector", lambda e: e.tensor_copy(stats[:, 7:8], consts[:, 128:129]),
          incs=((sVE, 1),))
        return sVE.n

    def sfence():
        E("scalar", lambda e: e.copy(stats[:, 7:8], consts[:, 128:129]),
          incs=((sSC, 1),))
        return sSC.n

    dW = p.sem("dW")
    dX = p.sem("dX")
    dKV = p.sem("dKV")
    sPE = p.sem("sPE")
    sSC = p.sem("sSC")
    sVE = p.sem("sVE")
    sGP = p.sem("sGP")

    E = p.emit

    identbf_ready = [0]
    h2dma = [0]

    def P0():
        E("sync", lambda e, d=identsb, s=identf: e.dma_start(out=d, in_=s), incs=((dKV, 16),))
        E("sync", lambda e, d=cbf[:, 0:512].rearrange("p (r q) -> p r q", r=4),
          s=masks.rearrange("r kk qq -> kk r qq"): e.dma_start(out=d, in_=s), incs=((dKV, 16),))
        E("sync", lambda e, d=battn_sb[:], s=b_attn.rearrange("(t p) -> p t", p=128):
          e.dma_start(out=d, in_=s, allow_slow_non_contiguous=True), incs=((dKV, 16),))
        E("sync", lambda e, d=bproj_sb[:], s=b_proj.rearrange("(t p) -> p t", p=128):
          e.dma_start(out=d, in_=s, allow_slow_non_contiguous=True), incs=((dKV, 16),))
        E("sync", lambda e, d=gateb_sb[:], s=gate_b.rearrange("(a b) -> a b", b=1):
          e.dma_start(out=d, in_=s, allow_slow_non_contiguous=True), incs=((dKV, 16),))
        E("gpsimd", lambda e, d=rowbf[0:1, 0:1024], s=b_attn[2048:3072].rearrange("(a b) -> a b", a=1):
          e.dma_start(out=d, in_=s), incs=((dW, 16),))
        E("vector", lambda e: e.memset(expbias, EXP_BIAS), incs=((sVE, 1),))
        E("vector", lambda e: e.memset(lnepsb, EPS * 1024.0))
        E("vector", lambda e: e.memset(halflnb, 0.5 * float(np.log(1024.0))))
        E("vector", lambda e: e.memset(rowbf[0:1, 1024:1152], 1.0))
        E("vector", lambda e: e.memset(rowf[0:1, 0:128], 1.0))
        E("vector", lambda e: e.memset(vst[:], 0.0))
        for bb in range(2):
            base = 1040 * bb
            E("vector", lambda e, b2=base: e.memset(
                vst[:, b2:b2 + 1040].rearrange("p (hp c) -> p hp c", hp=8)[:, :, 64:65], 1.0),
              incs=((sVE, 1),))
            E("vector", lambda e, b2=base: e.memset(
                vst[:, b2:b2 + 1040].rearrange("p (hp c) -> p hp c", hp=8)[:, :, 129:130], 1.0),
              incs=((sVE, 1),))
        E("vector", lambda e: e.memset(small[:, 0:64], 1.0), incs=((sVE, 1),))
        E("vector", lambda e: e.tensor_copy(identbf[:], identsb), waits=((dKV, 16),))
        identbf_ready[0] = vfence()

    cast_dw = {}

    def P0c():
        pace = ((dKV, dKV.n + (288 if STAGE >= 3 else 0)),)
        for e_ in range(17):
            if e_ == 0:
                srcs = (sh_wg, sh_wu, sh_wd)
            else:
                srcs = (ex_wg[e_ - 1], ex_wu[e_ - 1], ex_wd[e_ - 1])
            E("gpsimd", lambda e, d=wslab[e_, 0], s=srcs[0].rearrange("(ct p) i -> p ct i", p=128):
              e.dma_start(out=d.rearrange("p (ct i) -> p ct i", ct=8), in_=s),
              waits=(pace if e_ == 0 else ()), incs=((dW, 16),))
            E("gpsimd", lambda e, d=wslab[e_, 1], s=srcs[1].rearrange("(ct p) i -> p ct i", p=128):
              e.dma_start(out=d.rearrange("p (ct i) -> p ct i", ct=8), in_=s), incs=((dW, 16),))
            E("gpsimd", lambda e, d=wslab[e_, 2], s=srcs[2].rearrange("(ct p) d2 -> p ct d2", p=128):
              e.dma_start(out=d.rearrange("p (ct d2) -> p ct d2", ct=4), in_=s), incs=((dW, 16),))
            cast_dw[e_] = dW.n

    def WA(ct):
        return AW[:, ct * 3072:(ct + 1) * 3072]

    def P0_wattn():
        for ft in range(24):
            buf = wstg[:, (ft % 2) * 1024:(ft % 2) * 1024 + 1024]
            if ft >= 2:
                E("sync", None, waits=((sPE, p_done[ft - 2]),))
            E("sync", lambda e, d=buf, s=W_attn[ft * 128:(ft + 1) * 128, :]:
              e.dma_start(out=d, in_=s), incs=((dKV, 16),))
            v = dKV.n
            for ct in range(8):
                bk = ct % 2
                bank = ps[bk]
                E("tensor", lambda e, o=bank[:, 0:128], i=buf[:, ct * 128:(ct + 1) * 128]:
                  e.transpose(o, i, identsb),
                  waits=((dKV, v),) + bank_wait(bk), incs=((sPE, 1),))
                pe_v = sPE.n
                E("vector", lambda e, d=WA(ct)[:, ft * 128:(ft + 1) * 128], s=bank[:, 0:128]:
                  e.tensor_copy(d, s), waits=((sPE, pe_v),), incs=((sVE, 1),))
                bank_consumed(bk, sVE, sVE.n)
            p_done[ft] = sPE.n

    p_done = {}
    ev_done = {}

    ln_prev = {}
    def emit_ln_pair(xA, oA, xB, oB, n, wait_x=(), war_wait=()):
        stA = stats[:, 0:7]
        stB = stats[:, 8:15]
        wx = tuple(wait_x) + tuple(war_wait) + ((sSC, ln_prev.get(0, 0)),)
        E("scalar", lambda e, o=sqscr[:, 0:n], i=xA, acc=stA[:, 0:1]:
          e.activation(o, i, AF.Identity, bias=0.0, scale=1.0 / n, accum_out=acc),
          waits=wx)
        E("scalar", lambda e, o=sqscr[:, 0:n], i=xB, acc=stB[:, 0:1]:
          e.activation(o, i, AF.Identity, bias=0.0, scale=1.0 / n, accum_out=acc))
        E("scalar", lambda e, o=stA[:, 1:2], i=stA[:, 0:1]: e.mul(o, i, -1.0))
        E("scalar", lambda e, o=stB[:, 1:2], i=stB[:, 0:1]: e.mul(o, i, -1.0))
        E("scalar", lambda e, o=sqscr[:, 0:n], i=xA, bi=stA[:, 1:2], acc=stA[:, 2:3]:
          e.activation(o, i, AF.Square, bias=bi, scale=1.0, accum_out=acc))
        E("scalar", lambda e, o=sqscr[:, 0:n], i=xB, bi=stB[:, 1:2], acc=stB[:, 2:3]:
          e.activation(o, i, AF.Square, bias=bi, scale=1.0, accum_out=acc))
        E("scalar", lambda e, o=stA[:, 3:4], i=stA[:, 2:3]:
          e.activation(o, i, AF.Ln, bias=lnepsb, scale=1.0))
        E("scalar", lambda e, o=stB[:, 3:4], i=stB[:, 2:3]:
          e.activation(o, i, AF.Ln, bias=lnepsb, scale=1.0))
        E("scalar", lambda e, o=stA[:, 4:5], i=stA[:, 3:4]:
          e.activation(o, i, AF.Exp, bias=halflnb, scale=-0.5))
        E("scalar", lambda e, o=stB[:, 4:5], i=stB[:, 3:4]:
          e.activation(o, i, AF.Exp, bias=halflnb, scale=-0.5))
        E("scalar", lambda e, o=stA[:, 5:6], i=stA[:, 1:2], mm=stA[:, 4:5]: e.mul(o, i, mm))
        E("scalar", lambda e, o=stB[:, 5:6], i=stB[:, 1:2], mm=stB[:, 4:5]: e.mul(o, i, mm))
        E("scalar", lambda e, o=oA, i=xA, sc=stA[:, 4:5], bi=stA[:, 5:6]:
          e.activation(o, i, AF.Identity, bias=bi, scale=sc), incs=((sSC, 1),))
        E("scalar", lambda e, o=oB, i=xB, sc=stB[:, 4:5], bi=stB[:, 5:6]:
          e.activation(o, i, AF.Identity, bias=bi, scale=sc), incs=((sSC, 1),))
        ln_prev[0] = sSC.n
        return sSC.n

    def emit_transpose_cast(srctile, dest_fn, wait_sc, bankbase=6, extra_waits=()):
        for cp in range(4):
            bk = bankbase + (cp % 2)
            bank = ps[bk]
            for k in range(2):
                ct = 2 * cp + k
                E("tensor", lambda e, o=bank[:, k * 128:k * 128 + 128],
                  i=srctile[:, ct * 128:(ct + 1) * 128]:
                  e.matmul(o, i, identbf[:], start=True, stop=True),
                  waits=(((sSC, wait_sc), (sVE, identbf_ready[0])) + bank_wait(bk))
                        if k == 0 else (),
                  incs=(((sPE, 1),) if k == 1 else ()))
            pv = sPE.n
            E("vector", lambda e, d=dest_fn(cp),
              s=bank[:, 0:256].rearrange("p (c x) -> p c x", c=2):
              e.tensor_copy(d, s),
              waits=((sPE, pv),) + (tuple(extra_waits) if cp == 0 else ()),
              incs=((sVE, 1),))
            bank_consumed(bk, sVE, sVE.n)
        return sVE.n

    def XLN(ct):
        return AY[:, ct * 2048:(ct + 1) * 2048]

    def QT_(hp):
        return AQ[:, hp * 2048:(hp + 1) * 2048]

    def XLNp(cp, t2):
        return AY[:, 0:16384].rearrange("p (c x) -> p c x", c=8)[
            :, 2 * cp:2 * cp + 2, t2 * 128:(t2 + 1) * 128]

    def XLKp(cp, j):
        return xlk[:, 0:4096].rearrange("p (c x) -> p c x", c=8)[
            :, 2 * cp:2 * cp + 2, j * 128:(j + 1) * 128]

    def XL2p(cp, t2):
        return AQ[:, 0:16384].rearrange("p (c x) -> p c x", c=8)[
            :, 2 * cp:2 * cp + 2, t2 * 128:(t2 + 1) * 128]

    def P1():
        def emit_pair(tp):
            tA, tB = 2 * tp, 2 * tp + 1
            xb = xstg if tp % 2 == 0 else xstg2
            lb = lnstg if tp % 2 == 0 else lnstg2
            xbA = xb[:, 0:1024]
            xbB = xb[:, 1024:2048]
            lbA = lb[:, 0:1024]
            lbB = lb[:, 1024:2048]
            w = ((sVE, tr_done[tp - 2]),) if tp >= 2 else ()
            E("scalar", lambda e, d=xbA, s=x_own[tA * 128:(tA + 1) * 128, :]:
              e.dma_start(out=d, in_=s), waits=w, incs=((dX, 16),))
            E("scalar", lambda e, d=xbB, s=x_own[tB * 128:(tB + 1) * 128, :]:
              e.dma_start(out=d, in_=s), incs=((dX, 16),))
            vx = dX.n
            sc = emit_ln_pair(xbA, lbA, xbB, lbB, C, wait_x=((dX, vx),), war_wait=w)
            emit_transpose_cast(lbA, lambda cp, t2=tA: XLNp(cp, t2), sc)
            tr_done[tp] = emit_transpose_cast(
                lbB, lambda cp, t2=tB: XLNp(cp, t2), sc)

        def emit_qproj(ch):
            wlast = tr_done[2 * ch + 1]
            for hp in range(8):
                bk = (hp * 4 + ch) % 4
                bank = ps[bk]
                for ct in range(8):
                    E("tensor", lambda e, o=bank[:], l=WA(ct)[:, hp * 128:(hp + 1) * 128],
                      r=XLN(ct)[:, ch * 512:(ch + 1) * 512], st=(ct == 0), sp=(ct == 7):
                      e.matmul(o, l, r, start=st, stop=sp),
                      waits=(((sVE, wlast),) + bank_wait(bk)) if ct == 0 else (),
                      incs=(((sPE, 1),) if ct == 7 else ()))
                pv = sPE.n
                E("vector", lambda e, d=QT_(hp)[:, ch * 512:(ch + 1) * 512], s=bank[:],
                  bi=battn_sb[:, hp:hp + 1]:
                  e.tensor_scalar(d, s, bi, None, ALU.add),
                  waits=((sPE, pv),), incs=((sVE, 1),))
                bank_consumed(bk, sVE, sVE.n)
                qevac_done[hp * 4 + ch] = sVE.n

        for tp in range(NT // 2):
            emit_pair(tp)
            if tp >= 3 and tp % 2 == 1:
                emit_qproj((tp - 3) // 2)
        emit_qproj(3)

    tr_done = {}
    qevac_done = {}
    p2_mm_done = {}
    p2_pg0 = NT // 2
    p4_pg0 = NT // 2 + 32
    wp_done = {}
    spill_done = {}
    out_stg_free = {}

    def P1_dbg():
        if STAGE != 1:
            return
        E("scalar", None, waits=((sVE, qevac_done[31]),))
        for ct in range(8):
            E("scalar", lambda e, d=dbg["xlnT"][ct * 128:(ct + 1) * 128, :], s=XLN(ct):
              e.dma_start(out=d, in_=s), incs=((dX, 16),))
        for hp in range(8):
            E("scalar", lambda e, d=dbg["qT"][hp * 128:(hp + 1) * 128, :], s=QT_(hp):
              e.dma_start(out=d, in_=s), incs=((dX, 16),))
        E("scalar", lambda e, d=dbg["st"], s=small[:, 0:64]: e.dma_start(out=d, in_=s),
          incs=((dX, 16),))
        E("scalar", None, waits=((dX, dX.n),))
        E("scalar", lambda e, d=out_c[0:128, :], s=xstg[:, 0:1024]:
          e.dma_start(out=d, in_=s), incs=((dX, 16),))
        E("scalar", None, waits=((dX, dX.n),))

    def XLK(ct):
        return xlk[:, ct * 512:(ct + 1) * 512]

    def P2():
        kvev_free = {}
        vst_free = {}
        ln_sc = {}

        def emit_ln(cc):
            for pr in range(2):
                tA = 4 * cc + 2 * pr
                tB = tA + 1
                pg = p2_pg0 + 2 * cc + pr
                xb = xstg if pg % 2 == 0 else xstg2
                lb = lnstg if pg % 2 == 0 else lnstg2
                xbA = xb[:, 0:1024]
                xbB = xb[:, 1024:2048]
                lbA = lb[:, 0:1024]
                lbB = lb[:, 1024:2048]
                w = ((sVE, tr_done[pg - 2]),) if (pg - 2) in tr_done else ()
                E("scalar", lambda e, d=xbA, s=x_kv[tA * 128:(tA + 1) * 128, :]:
                  e.dma_start(out=d, in_=s), waits=w, incs=((dX, 16),))
                E("scalar", lambda e, d=xbB, s=x_kv[tB * 128:(tB + 1) * 128, :]:
                  e.dma_start(out=d, in_=s), incs=((dX, 16),))
                vx = dX.n
                ln_sc[(cc, pr)] = emit_ln_pair(xbA, lbA, xbB, lbB, C,
                                               wait_x=((dX, vx),), war_wait=w)

        def emit_tile_tr(cc, j):
            pr = j // 2
            pg = p2_pg0 + 2 * cc + pr
            lb = lnstg if pg % 2 == 0 else lnstg2
            lbt = lb[:, (j % 2) * 1024:(j % 2) * 1024 + 1024]
            v = emit_transpose_cast(
                lbt, lambda cp, j2=j: XLKp(cp, j2), ln_sc[(cc, pr)])
            if j % 2 == 1:
                tr_done[pg] = v

        def emit_K(cc):
            wx = tr_done[p2_pg0 + 2 * cc + 1]
            for kf in range(8):
                bk = kf % 4
                bank = ps[bk]
                for ct in range(8):
                    E("tensor", lambda e, o=bank[:], l=WA(ct)[:, 1024 + kf * 128:1024 + (kf + 1) * 128],
                      r=XLK(ct), st=(ct == 0), sp=(ct == 7):
                      e.matmul(o, l, r, start=st, stop=sp),
                      waits=(((sVE, wx),) + bank_wait(bk)) if ct == 0 else (),
                      incs=(((sPE, 1),) if ct == 7 else ()))
                pv = sPE.n
                kb = kf % 2
                kbuf = kvev[:, kb * 512:(kb + 1) * 512]
                E("vector", lambda e, d=kbuf, s=bank[:], bi=battn_sb[:, 8 + kf:9 + kf]:
                  e.tensor_scalar(d, s, bi, None, ALU.add),
                  waits=((sPE, pv), (dKV, kvev_free.get(kb, 0))), incs=((sVE, 1),))
                bank_consumed(bk, sVE, sVE.n)
                scv = sVE.n
                E("sync", lambda e, d=kslab[kf, :, cc * 512:(cc + 1) * 512], s=kbuf:
                  e.dma_start(out=d, in_=s), waits=((sVE, scv),), incs=((dKV, 16),))
                kvev_free[kb] = dKV.n

        def emit_V(cc, tj):
            bk = 4 + (tj % 2)
            bank = ps[bk]
            vb = tj % 2
            vbuf = vst[:, vb * 1040:(vb + 1) * 1040]
            for vc in range(2):
                for ct in range(8):
                    E("tensor", lambda e, o=bank[:], l=XLK(ct)[:, tj * 128:(tj + 1) * 128],
                      r=WA(ct)[:, 2048 + vc * 512:2048 + (vc + 1) * 512],
                      st=(ct == 0), sp=False:
                      e.matmul(o, l, r, start=st, stop=sp),
                      waits=(bank_wait(bk) if ct == 0 else ()),
                      incs=())
                E("tensor", lambda e, o=bank[:], l=rowbf[0:1, 1024:1152],
                  r=rowbf[0:1, vc * 512:(vc + 1) * 512]:
                  e.matmul(o, l, r, start=False, stop=True),
                  incs=((sPE, 1),))
                pv = sPE.n
                pv_s = bank[:].rearrange("p (hp hh d) -> p hp hh d", hp=4, hh=2)
                E("vector", lambda e, s=pv_s[:, :, 0, :],
                  d=vbuf.rearrange("p (hp y) -> p hp y", hp=8)[:, 4 * vc:4 * vc + 4, 0:64]:
                  e.tensor_copy(d, s),
                  waits=((sPE, pv), (dKV, vst_free.get(vb, 0))))
                E("vector", lambda e, s=pv_s[:, :, 1, :],
                  d=vbuf.rearrange("p (hp y) -> p hp y", hp=8)[:, 4 * vc:4 * vc + 4, 65:129]:
                  e.tensor_copy(d, s), incs=((sVE, 1),))
                bank_consumed(bk, sVE, sVE.n)
            vv = sVE.n
            E("sync", lambda e,
              d=vslab[:, cc * 512 + tj * 128: cc * 512 + (tj + 1) * 128, :]
                .rearrange("hp r y -> r hp y"),
              s=vbuf.rearrange("p (hp y) -> p hp y", hp=8):
              e.dma_start(out=d, in_=s), waits=((sVE, vv),), incs=((dKV, 16),))
            vst_free[vb] = dKV.n

        for cc in range(16):
            emit_ln(cc)
            if cc == 0:
                for j in range(4):
                    emit_tile_tr(0, j)
                continue
            emit_K(cc - 1)
            for tj in range(4):
                emit_V(cc - 1, tj)
                emit_tile_tr(cc, tj)
        emit_K(15)
        for tj in range(4):
            emit_V(15, tj)
        p2_mm_done[15] = sPE.n

    def P2_dbg():
        if STAGE != 2:
            return
        E("sync", None, waits=((dKV, dKV.n),))
        E("sync", lambda e, d=dbg["kslab"], s=kslab: e.dma_start(out=d, in_=s),
          incs=((dKV, 16),))
        E("sync", lambda e, d=dbg["vslab"], s=vslab: e.dma_start(out=d, in_=s),
          incs=((dKV, 16),))
        E("sync", None, waits=((dKV, dKV.n),))
        E("scalar", lambda e, d=out_c[0:128, :], s=xstg[:, 0:1024]:
          e.dma_start(out=d, in_=s), incs=((dX, 16),))
        E("scalar", None, waits=((dX, dX.n),))


    def P3():
        kvbuf = [(AK[:, 0:8192], AK[:, 8192:16512]),
                 (AW[:, 0:8192], AW[:, 8192:16512])]
        kv_ready = {}
        hp_mm_done = {}
        p2_all = dKV.n
        ybank = {(0, 0): 4, (0, 1): 5, (1, 0): 6, (1, 1): 7}
        pending = []
        norm_queue = []
        rw_set = [0, 0, 0, 0]
        estep = [0]

        def emit_av(avd):
            (hp2, par2, pvexp, mvs, avlist, kvb2) = avd
            for (h, yb, co, w2, tile, st_flag, sp_flag) in avlist:
                wts2 = [(sSC, pvexp)]
                if mvs is not None:
                    wts2.append((sVE, mvs))
                if st_flag:
                    wts2 += list(bank_wait(yb))
                E("tensor", lambda e, o=ps[yb][0:65, co:co + w2],
                  l=kvb2[:, tile * 130 + 65 * h: tile * 130 + 65 * h + 65],
                  r2=eT[:, par2 * 1024 + 512 * h: par2 * 1024 + 512 * h + w2],
                  st=st_flag, sp=sp_flag:
                  e.matmul(o, l, r2, start=st, stop=sp, skip_group_check=True),
                  waits=tuple(wts2), incs=((sPE, 1),))
                if sp_flag:
                    qc2 = {4: 0, 5: 1, 6: 0, 7: 1}[yb]
                    si = 2 * h + qc2
                    pvs2 = sPE.n
                    E("scalar", lambda e, o=oev[64:65, si * 512:(si + 1) * 512],
                      i=ps[yb][64:65, 0:512]:
                      e.activation(o, i, AF.Ln, bias=0.0, scale=1.0),
                      waits=((sPE, pvs2),))
                    E("scalar", lambda e, o=oev[64:65, si * 512:(si + 1) * 512],
                      i=oev[64:65, si * 512:(si + 1) * 512]: e.mul(o, i, -1.0))
                    E("scalar", lambda e, o=ofin[64:65, si * 512:(si + 1) * 512],
                      i=oev[64:65, si * 512:(si + 1) * 512]:
                      e.activation(o, i, AF.Exp, bias=0.0, scale=1.0), incs=((sSC, 1),))
                    rw_set[si] = sSC.n

        def flush_norm():
            while norm_queue:
                norm_queue.pop(0)()

        for hp in range(NHP):
            par = hp % 2
            kbuf, vbuf = kvbuf[par]
            bw = ((sPE, hp_mm_done.get(hp - 2, 0)),)
            E("sync", lambda e, d=kbuf, s=kslab[hp]: e.dma_start(out=d, in_=s),
              waits=((dKV, p2_all),) + bw, incs=((dKV, 16),))
            for tg in range(8):
                E("sync", lambda e,
                  d=vbuf.rearrange("p (tl y) -> p tl y", tl=64)[:, tg * 8:(tg + 1) * 8, :],
                  s=vslab[hp, tg * 1024:(tg + 1) * 1024, :].rearrange("(tl p) y -> p tl y", p=128):
                  e.dma_start(out=d, in_=s), incs=((dKV, 16),), dma_selfwait=False)
            kv_ready[hp] = dKV.n
            for qg in range(2):
                sched = []
                for a in range(8 * qg + 8):
                    su0 = max(qg * 1024, 128 * a)
                    su1 = qg * 1024 + 1024
                    subs = []
                    s0 = su0
                    nxt = 512 * (s0 // 512) + 512
                    while s0 < su1:
                        s1 = min(su1, nxt)
                        subs.append((s0, s1))
                        s0, nxt = s1, s1 + 512
                    for r in range(4):
                        for (s0b, s1b) in subs:
                            sched.append((a, r, s0b, s1b))
                last_touch = {}
                for idx, (a, r, s0b, s1b) in enumerate(sched):
                    qc = (s0b - qg * 1024) // 512
                    for h in range(2):
                        last_touch[(h, qc)] = idx
                started = set()
                for idx, (a, r, s0b, s1b) in enumerate(sched):
                    w = s1b - s0b
                    qc = (s0b - qg * 1024) // 512
                    co = s0b % 512
                    boundary = (a >= 8 * qg) and (s0b == max(qg * 1024, 128 * a))
                    pp = estep[0] % 2
                    estep[0] += 1
                    sD = psD[pp]
                    for h in range(2):
                        bkid = 2 * pp + h
                        E("tensor", lambda e, o=sD[:, 512 * h:512 * h + w],
                          l=kbuf[64 * h:64 * h + 64, a * 512 + r * 128:a * 512 + r * 128 + 128],
                          r2=QT_(hp)[64 * h:64 * h + 64, s0b:s1b], tp=(64 * h, 0):
                          e.matmul(o, l, r2, start=True, stop=True, tile_position=tp),
                          waits=((dKV, kv_ready[hp]),) + bank_wait(bkid),
                          incs=(((sPE, 1),) if h == 1 else ()))
                    pvs = sPE.n
                    eslot = eT[:, pp * 1024: pp * 1024 + 1024]
                    ein = sD[:].rearrange("p (h2 q) -> p h2 q", h2=2)[:, :, 0:w]
                    eout = eslot.rearrange("p (h2 q) -> p h2 q", h2=2)[:, :, 0:w]
                    E("scalar", lambda e, o=eout, i=ein:
                      e.activation(o, i, AF.Exp, bias=expbias, scale=0.125),
                      waits=((sPE, pvs),), incs=((sSC, 1),))
                    bank_consumed(2 * pp, sSC, sSC.n)
                    bank_consumed(2 * pp + 1, sSC, sSC.n)
                    pvexp = sSC.n
                    mvs = None
                    if boundary:
                        E("vector", lambda e, o=eslot[:, 0:128], m=cbf[:, r * 128:(r + 1) * 128]:
                          e.tensor_mul(o, o, m), waits=((sSC, pvexp),))
                        E("vector", lambda e, o=eslot[:, 512:640], m=cbf[:, r * 128:(r + 1) * 128]:
                          e.tensor_mul(o, o, m), incs=((sVE, 1),))
                        mvs = sVE.n
                    if idx == 1:
                        flush_norm()
                    if pending:
                        emit_av(pending.pop(0))
                    avlist = []
                    for h in range(2):
                        yb = ybank[(h, qc)]
                        st_flag = (h, qc) not in started
                        started.add((h, qc))
                        sp_flag = last_touch[(h, qc)] == idx
                        avlist.append((h, yb, co, w, a * 4 + r, st_flag, sp_flag))
                    pending.append((hp, pp, pvexp, mvs, avlist, vbuf))
                while pending:
                    emit_av(pending.pop(0))
                pv_all = sPE.n

                def make_norm(hp=hp, qg=qg, pv_all=pv_all, vbuf=vbuf,
                              rws=tuple(rw_set)):
                    def go():
                        sets = [(h, qc) for h in range(2) for qc in range(2)]
                        for si, (h, qc) in enumerate(sets):
                            yb = ybank[(h, qc)]
                            rbk = si % 2
                            E("tensor", lambda e, o=ps[rbk][0:64, 0:512],
                              l=small[64:65, 0:64], r2=ofin[64:65, si * 512:(si + 1) * 512]:
                              e.matmul(o, l, r2, start=True, stop=True),
                              waits=((sSC, rws[si]),) + bank_wait(rbk), incs=((sPE, 1),))
                            pv = sPE.n
                            E("vector", lambda e, o=t1s[0:64, 0:512], s=ps[rbk][0:64, 0:512]:
                              e.tensor_copy(o, s), waits=((sPE, pv),), incs=((sVE, 1),))
                            bank_consumed(rbk, sVE, sVE.n)
                            if h == 0:
                                dst = AY[0:64, hp * 2048 + qg * 1024 + qc * 512:
                                         hp * 2048 + qg * 1024 + qc * 512 + 512]
                            else:
                                dst = kvev[0:64, qc * 512:(qc + 1) * 512]
                            kw = ((dX, h2dma[0]),) if h == 1 else ()
                            E("vector", lambda e, o=dst, a2=ps[yb][0:64, 0:512],
                              b2=t1s[0:64, 0:512]:
                              e.tensor_mul(o, a2, b2), waits=kw, incs=((sVE, 1),))
                            bank_consumed(yb, sVE, sVE.n)
                        vv = sVE.n
                        E("scalar", lambda e,
                          o=AY[64:128, hp * 2048 + qg * 1024: hp * 2048 + qg * 1024 + 1024],
                          s=kvev[0:64, 0:1024]: e.dma_start(out=o, in_=s),
                          waits=((sVE, vv),), incs=((dX, 16),))
                        h2dma[0] = dX.n
                    return go
                norm_queue.append(make_norm())
            hp_mm_done[hp] = sPE.n
        flush_norm()
        E("tensor", None, waits=((dKV, dKV.n),))

    def P3_dbg():
        if STAGE != 3:
            return
        E("scalar", None, waits=((dKV, dKV.n), (dX, dX.n), (sVE, sVE.n)))
        for hp in range(8):
            E("scalar", lambda e, d=dbg["yT"][hp * 128:(hp + 1) * 128, :], s=AY[:, hp * 2048:(hp + 1) * 2048]:
              e.dma_start(out=d, in_=s), incs=((dX, 16),))
        E("scalar", lambda e, d=out_c[0:128, :], s=xstg[:, 0:1024]:
          e.dma_start(out=d, in_=s), incs=((dX, 16),))
        E("scalar", None, waits=((dX, dX.n),))

    def WPJ(ct):
        return AK[:, ct * 1024:(ct + 1) * 1024]

    def P4():
        for ft in range(8):
            buf = xstg[:, (ft % 2) * 1024:(ft % 2) * 1024 + 1024]
            wv = ((sPE, wp_done[ft - 2]),) if ft >= 2 else ()
            E("sync", lambda e, d=buf, s=W_proj[ft * 128:(ft + 1) * 128, :]:
              e.dma_start(out=d, in_=s), waits=wv, incs=((dKV, 16),))
            v = dKV.n
            for ct in range(8):
                bk = 6 + (ct % 2)
                bank = ps[bk]
                E("tensor", lambda e, o=bank[:, 0:128], i=buf[:, ct * 128:(ct + 1) * 128]:
                  e.transpose(o, i, identsb),
                  waits=((dKV, v),) + bank_wait(bk), incs=((sPE, 1),))
                pv = sPE.n
                E("vector", lambda e, d=WPJ(ct)[:, ft * 128:(ft + 1) * 128], s=bank[:, 0:128]:
                  e.tensor_copy(d, s), waits=((sPE, pv),), incs=((sVE, 1),))
                bank_consumed(bk, sVE, sVE.n)
            wp_done[ft] = sPE.n
        vfence()
        wpj_ready = sVE.n
        for ch in range(4):
            ev = {}
            for wave in range(2):
                stage_t = oev if wave == 0 else ofin
                for oti in range(4):
                    ot = 4 * wave + oti
                    bk = oti
                    for ct in range(8):
                        E("tensor", lambda e, o=ps[bk][:],
                          l=WPJ(ct)[:, ot * 128:(ot + 1) * 128],
                          r2=AY[:, ct * 2048 + ch * 512: ct * 2048 + (ch + 1) * 512],
                          st=(ct == 0), sp=(ct == 7):
                          e.matmul(o, l, r2, start=st, stop=sp),
                          waits=(((sVE, wpj_ready), (dX, h2dma[0])) + bank_wait(bk))
                                if ct == 0 else (),
                          incs=(((sPE, 1),) if ct == 7 else ()))
                    pv = sPE.n
                    E("scalar", lambda e, d=stage_t[:, oti * 512:(oti + 1) * 512],
                      s=ps[bk][:], bi=bproj_sb[:, ot:ot + 1]:
                      e.activation(d, s, AF.Identity, bias=bi, scale=1.0),
                      waits=((sPE, pv),), incs=((sSC, 1),))
                    bank_consumed(bk, sSC, sSC.n)
            sfence()
            ev_done2 = sSC.n
            for tt4 in range(4):
                tt = 4 * ch + tt4
                sl = tt4 % 2
                xb = xstg[:, sl * 1024:(sl + 1) * 1024]
                wv = ((dX, spill_done[tt - 2]),) if (tt - 2) in spill_done else ()
                E("scalar", lambda e, d=xb, s=x_own[tt * 128:(tt + 1) * 128, :]:
                  e.dma_start(out=d, in_=s), waits=wv, incs=((dX, 16),))
                vx = dX.n
                for ct in range(8):
                    stage_t = oev if ct < 4 else ofin
                    bk = 4 + ct // 4
                    E("tensor", lambda e, o=ps[bk][:, (ct % 4) * 128:(ct % 4) * 128 + 128],
                      i=stage_t[:, (ct % 4) * 512 + tt4 * 128:(ct % 4) * 512 + (tt4 + 1) * 128]:
                      e.transpose(o, i, identsb),
                      waits=((sSC, ev_done2),) + (bank_wait(bk) if ct % 4 == 0 else ()),
                      incs=((sPE, 1),))
                pv = sPE.n
                E("vector", lambda e, o=xb[:, 0:512], a2=ps[4][:], b2=xb[:, 0:512]:
                  e.tensor_add(o, a2, b2), waits=((sPE, pv), (dX, vx)))
                E("vector", lambda e, o=xb[:, 512:1024], a2=ps[5][:], b2=xb[:, 512:1024]:
                  e.tensor_add(o, a2, b2), incs=((sVE, 1),))
                bank_consumed(4, sVE, sVE.n)
                bank_consumed(5, sVE, sVE.n)
                av = sVE.n
                E("scalar", lambda e, d=x2sp[tt * 128:(tt + 1) * 128, :], s=xb:
                  e.dma_start(out=d, in_=s), waits=((sVE, av),), incs=((dX, 16),))
                spill_done[tt] = dX.n
        for tp in range(NT // 2):
            tA, tB = 2 * tp, 2 * tp + 1
            pg = p4_pg0 + tp
            xb = xstg if tp % 2 == 0 else xstg2
            lb = lnstg if tp % 2 == 0 else lnstg2
            xbA, xbB = xb[:, 0:1024], xb[:, 1024:2048]
            lbA, lbB = lb[:, 0:1024], lb[:, 1024:2048]
            w = ((sVE, tr_done[pg - 2]),) if tp >= 2 else \
                ((dX, spill_done[15]),)
            E("scalar", lambda e, d=xbA, s=x2sp[tA * 128:(tA + 1) * 128, :]:
              e.dma_start(out=d, in_=s), waits=w, incs=((dX, 16),))
            E("scalar", lambda e, d=xbB, s=x2sp[tB * 128:(tB + 1) * 128, :]:
              e.dma_start(out=d, in_=s), incs=((dX, 16),))
            vx = dX.n
            sc = emit_ln_pair(xbA, lbA, xbB, lbB, C, wait_x=((dX, vx),), war_wait=w)
            emit_transpose_cast(lbA, lambda cp, t2=tA: XL2p(cp, t2), sc)
            tr_done[pg] = emit_transpose_cast(
                lbB, lambda cp, t2=tB: XL2p(cp, t2), sc)

    def XL2(ct):
        return AQ[:, ct * 2048:(ct + 1) * 2048]

    def P4_dbg():
        if STAGE != 4:
            return
        E("scalar", None, waits=((sVE, tr_done[p4_pg0 + 7]), (dX, dX.n)))
        E("scalar", lambda e, d=dbg["x2"], s=x2sp: e.dma_start(out=d, in_=s),
          incs=((dX, 16),))
        for ct in range(8):
            E("scalar", lambda e, d=dbg["xln2T"][ct * 128:(ct + 1) * 128, :], s=XL2(ct):
              e.dma_start(out=d, in_=s), incs=((dX, 16),))
        E("scalar", lambda e, d=out_c[0:128, :], s=xstg[:, 0:1024]:
          e.dma_start(out=d, in_=s), incs=((dX, 16),))
        E("scalar", None, waits=((dX, dX.n),))

    def P5():
        gw = kvev[:, 0:128].rearrange("p (ct e2) -> p ct e2", ct=8)
        E("gpsimd", lambda e, d=gw, s=gate_w.rearrange("(ct p) e2 -> p ct e2", p=128):
          e.dma_start(out=d, in_=s),
          waits=((dKV, dKV.n), (sVE, sVE.n)), incs=((dW, 16),))
        gwv = dW.n
        for ch in range(4):
            bk = ch % 2
            for ct in range(8):
                E("tensor", lambda e, o=ps[bk][0:16, 0:512], l=gw[:, ct, :],
                  r2=XL2(ct)[:, ch * 512:(ch + 1) * 512], st=(ct == 0), sp=(ct == 7):
                  e.matmul(o, l, r2, start=st, stop=sp),
                  waits=(((dW, gwv), (sVE, tr_done[p4_pg0 + 7])) + bank_wait(bk))
                        if ct == 0 else (),
                  incs=(((sPE, 1),) if ct == 7 else ()))
                pv = sPE.n
            E("scalar", lambda e, d=rT16[0:16, ch * 512:(ch + 1) * 512],
              s=ps[bk][0:16, 0:512], bi=gateb_sb[0:16, 0:1]:
              e.activation(d, s, AF.Sigmoid, bias=bi, scale=1.0),
              waits=((sPE, pv),), incs=((sSC, 1),))
            bank_consumed(bk, sSC, sSC.n)
        sfence()
        rfv = sSC.n
        for tg in range(4):
            tls = [4 * tg + i for i in range(4)]
            for i, t in enumerate(tls):
                bk = 2 + i % 2
                E("tensor", lambda e, o=ps[bk][0:128, 0:16],
                  i2=rT16[0:16, t * 128:(t + 1) * 128], idn=identsb[0:16, 0:16]:
                  e.transpose(o, i2, idn),
                  waits=((sSC, rfv),) + bank_wait(bk), incs=((sPE, 1),))
                pv = sPE.n
                E("vector", lambda e, o=rtok[:, 16 * i:16 * i + 16], s=ps[bk][0:128, 0:16]:
                  e.tensor_copy(o, s), waits=((sPE, pv),), incs=((sVE, 1),))
                bank_consumed(bk, sVE, sVE.n)
            for i in range(4):
                E("vector", lambda e, o=small[:, 64 + i:65 + i], s=rtok[:, 16 * i:16 * i + 16]:
                  e.reduce_max(o, s, axis=AX))
            for i in range(4):
                E("vector", lambda e, o=wsl[:, 16 * i:16 * i + 16],
                  s=rtok[:, 16 * i:16 * i + 16], m=small[:, 64 + i:65 + i]:
                  e.tensor_scalar(o, s, m, None, ALU.is_equal))
            for i in range(4):
                E("vector", lambda e, o=r2sl[:, 16 * i:16 * i + 16],
                  a2=wsl[:, 16 * i:16 * i + 16], s=rtok[:, 16 * i:16 * i + 16]:
                  e.scalar_tensor_tensor(o, a2, -1e30, s, ALU.mult, ALU.add))
            for i in range(4):
                E("vector", lambda e, o=small[:, 68 + i:69 + i], s=r2sl[:, 16 * i:16 * i + 16]:
                  e.reduce_max(o, s, axis=AX))
            for i in range(4):
                E("vector", lambda e, o=r2sl[:, 16 * i:16 * i + 16],
                  s=r2sl[:, 16 * i:16 * i + 16], m=small[:, 68 + i:69 + i]:
                  e.tensor_scalar(o, s, m, None, ALU.is_equal))
            for i in range(4):
                E("vector", lambda e, o=wsl[:, 16 * i:16 * i + 16],
                  a2=wsl[:, 16 * i:16 * i + 16], b2=r2sl[:, 16 * i:16 * i + 16]:
                  e.tensor_add(o, a2, b2))
            for i in range(4):
                E("vector", lambda e, o=small[:, 72 + i:73 + i],
                  a2=small[:, 64 + i:65 + i], b2=small[:, 68 + i:69 + i]:
                  e.tensor_add(o, a2, b2))
            vfence()
            vv = sVE.n
            for i in range(4):
                E("scalar", lambda e, o=small[:, 76 + i:77 + i], s=small[:, 72 + i:73 + i]:
                  e.activation(o, s, AF.Ln, bias=0.0, scale=1.0), waits=((sVE, vv),))
            for i in range(4):
                E("scalar", lambda e, o=small[:, 76 + i:77 + i], s=small[:, 76 + i:77 + i]:
                  e.mul(o, s, -1.0))
            for i in range(4):
                E("scalar", lambda e, o=small[:, 80 + i:81 + i], s=small[:, 76 + i:77 + i]:
                  e.activation(o, s, AF.Exp, bias=0.0, scale=1.0), incs=((sSC, 1),))
            sfence()
            sv = sSC.n
            for i in range(4):
                E("vector", lambda e, o=wsl[:, 16 * i:16 * i + 16],
                  a2=wsl[:, 16 * i:16 * i + 16], b2=rtok[:, 16 * i:16 * i + 16]:
                  e.tensor_mul(o, a2, b2), waits=((sSC, sv),))
            for i in range(4):
                E("vector", lambda e, o=wsl[:, 16 * i:16 * i + 16],
                  s=wsl[:, 16 * i:16 * i + 16], m=small[:, 80 + i:81 + i]:
                  e.tensor_scalar(o, s, m, None, ALU.mult), incs=((sVE, 1),))
            vfence()
            wv2 = sVE.n
            for i, t in enumerate(tls):
                bk = 2 + i % 2
                E("tensor", lambda e, o=ps[bk][0:16, 0:128],
                  i2=wsl[:, 16 * i:16 * i + 16], idn=identsb:
                  e.transpose(o, i2, idn),
                  waits=((sVE, wv2),) + bank_wait(bk), incs=((sPE, 1),))
                pv = sPE.n
                E("vector", lambda e, o=wT16[0:16, t * 128:(t + 1) * 128], s=ps[bk][0:16, 0:128]:
                  e.tensor_copy(o, s), waits=((sPE, pv),), incs=((sVE, 1),))
                bank_consumed(bk, sVE, sVE.n)
        vfence()

    def P5_dbg():
        if STAGE != 5:
            return
        E("scalar", None, waits=((sVE, sVE.n),))
        E("scalar", lambda e, d=dbg["wT"], s=wT16[0:16, :]: e.dma_start(out=d, in_=s),
          incs=((dX, 16),))
        E("scalar", lambda e, d=dbg["rT"], s=rT16[0:16, :]: e.dma_start(out=d, in_=s),
          incs=((dX, 16),))
        E("scalar", lambda e, d=dbg["rtok"], s=rtok[:, 0:64]: e.dma_start(out=d, in_=s),
          incs=((dX, 16),))
        E("scalar", lambda e, d=out_c[0:128, :], s=xstg[:, 0:1024]:
          e.dma_start(out=d, in_=s), incs=((dX, 16),))
        E("scalar", None, waits=((dX, dX.n),))

    def P6():
        E("gpsimd", lambda e, d=eT[0:16, 0:2048], s=selb: e.dma_start(out=d, in_=s),
          waits=((sPE, sPE.n),), incs=((dW, 16),))
        selv = dW.n
        comb_guard = {}
        out_done = {}
        wload = {}
        wfree = {-2: sPE.n, -1: sPE.n}
        hp_free = {}
        last_t1 = [0]
        ffw_free = {}
        acc_add = {}
        stg_free = {}

        def emit_combine(ch):
            c0 = ch * 256
            slot = ch % 2
            stg = oev if slot == 0 else ofin
            sw = comb_guard.get(slot, ())
            E("sync", lambda e,
              d=stg[:, 0:2048].rearrange("p (ot t2) -> p ot t2", ot=8),
              s=ffacc.rearrange("(ot p) t2 -> p ot t2", p=128)[:, :, c0:c0 + 256]:
              e.dma_start(out=d, in_=s),
              waits=((dW, acc_add[(16, ch)]),) + sw, incs=((dKV, 16),))
            lv = dKV.n
            for ot in range(8):
                E("vector", lambda e, o=stg[:, ot * 256:(ot + 1) * 256],
                  a2=stg[:, ot * 256:(ot + 1) * 256], b2=XL2(ot)[:, c0:c0 + 256]:
                  e.tensor_add(o, a2, b2),
                  waits=((dKV, lv),), incs=(((sVE, 1),) if ot == 7 else ()))
            vfence()
            adv = sVE.n
            for tt2 in range(2):
                tt = 2 * ch + tt2
                sl = tt2
                xb = xstg[:, sl * 1024:(sl + 1) * 1024]
                wv2 = ((dX, out_done[tt - 2]),) if (tt - 2) in out_done else ()
                E("scalar", lambda e, d=xb, s=x2sp[tt * 128:(tt + 1) * 128, :]:
                  e.dma_start(out=d, in_=s), waits=wv2, incs=((dX, 16),))
                vx = dX.n
                for ct in range(8):
                    bk = 4 + ct // 4
                    E("tensor", lambda e, o=ps[bk][:, (ct % 4) * 128:(ct % 4) * 128 + 128],
                      i=stg[:, ct * 256 + tt2 * 128: ct * 256 + (tt2 + 1) * 128]:
                      e.transpose(o, i, identsb),
                      waits=((sVE, adv),) + (bank_wait(bk) if ct % 4 == 0 else ()),
                      incs=((sPE, 1),))
                pv = sPE.n
                fb = hprime[:, 0:1024].rearrange("p x -> p x") if False else None
                E("vector", lambda e, o=xb[:, 0:512], a2=ps[4][:], b2=xb[:, 0:512]:
                  e.tensor_add(o, a2, b2), waits=((sPE, pv), (dX, vx)))
                E("vector", lambda e, o=xb[:, 512:1024], a2=ps[5][:], b2=xb[:, 512:1024]:
                  e.tensor_add(o, a2, b2), incs=((sVE, 1),))
                bank_consumed(4, sVE, sVE.n)
                bank_consumed(5, sVE, sVE.n)
                fv = sVE.n
                E("scalar", lambda e, d=out_c[tt * 128:(tt + 1) * 128, :], s=xb:
                  e.dma_start(out=d, in_=s), waits=((sVE, fv),), incs=((dX, 16),))
                out_done[tt] = dX.n
            comb_guard[slot] = ((sPE, pv),)

        for e_ in range(17):
            par = e_ % 2
            base = par * 12288
            wv = ((sPE, wfree.get(e_ - 2, 0)), (dW, cast_dw[16]))
            for pi in range(3):
                off = (0, 4096, 8192)[pi]
                E("sync", lambda e, d=AW[:, base + off:base + off + 4096], s=wslab[e_, pi]:
                  e.dma_start(out=d, in_=s),
                  waits=wv if pi == 0 else (), incs=((dKV, 16),),
                  dma_selfwait=(pi == 0))
            wload[e_] = dKV.n
            if e_ > 0:
                fsl = ffw[:, (e_ % 2) * 2048:(e_ % 2) * 2048 + 2048]
                for ch in range(8):
                    bk = 4 + ch % 2
                    E("tensor", lambda e, o=ps[bk][0:128, 0:256],
                      l=eT[0:16, (e_ - 1) * 128:e_ * 128],
                      r2=wT16[0:16, ch * 256:(ch + 1) * 256]:
                      e.matmul(o, l, r2, start=True, stop=True),
                      waits=((dW, selv), (sVE, ffw_free.get(e_ - 2, 0))) + bank_wait(bk),
                      incs=((sPE, 1),))
                    pv = sPE.n
                    E("vector", lambda e, o=fsl[:, ch * 256:(ch + 1) * 256], s=ps[bk][0:128, 0:256]:
                      e.tensor_copy(o, s), waits=((sPE, pv),), incs=((sVE, 1),))
                    bank_consumed(bk, sVE, sVE.n)
                vfence()
                ffw_free[e_] = sVE.n
            for ch in range(8):
                c0 = ch * 256
                pvgu = {}
                for hf in range(2):
                    gb, ub = (4, 5) if hf == 0 else (6, 7)
                    for mt in range(2):
                        for ct in range(8):
                            E("tensor", lambda e, o=ps[gb][:, mt * 256:(mt + 1) * 256],
                              l=AW[:, base + ct * 512 + hf * 256 + mt * 128:
                                    base + ct * 512 + hf * 256 + mt * 128 + 128],
                              r2=XL2(ct)[:, c0:c0 + 256], st=(ct == 0), sp=(ct == 7):
                              e.matmul(o, l, r2, start=st, stop=sp),
                              waits=(((dKV, wload[e_]),) + bank_wait(gb)) if (mt == 0 and ct == 0) else (),
                              incs=())
                    for mt in range(2):
                        for ct in range(8):
                            E("tensor", lambda e, o=ps[ub][:, mt * 256:(mt + 1) * 256],
                              l=AW[:, base + 4096 + ct * 512 + hf * 256 + mt * 128:
                                    base + 4096 + ct * 512 + hf * 256 + mt * 128 + 128],
                              r2=XL2(ct)[:, c0:c0 + 256], st=(ct == 0), sp=(ct == 7):
                              e.matmul(o, l, r2, start=st, stop=sp),
                              waits=(bank_wait(ub)) if (mt == 0 and ct == 0) else (),
                              incs=(((sPE, 1),) if (mt == 1 and ct == 7) else ()))
                    pvgu[hf] = sPE.n
                for hf in range(2):
                    gb, ub = (4, 5) if hf == 0 else (6, 7)
                    pv = pvgu[hf]
                    hw = ((sPE, hp_free.get((e_, ch - 1, hf), hp_free.get((e_ - 1, 7, hf), 0))),)
                    E("scalar", lambda e, o=t1s[:, 0:512], s=ps[gb][:]:
                      e.activation(o, s, AF.Silu, bias=0.0, scale=1.0),
                      waits=((sPE, pv), (sVE, last_t1[0])) + hw, incs=((sSC, 1),))
                    scv = sSC.n
                    hslot = hprime[:, hf * 512:(hf + 1) * 512]
                    E("vector", lambda e, o=hslot, a2=t1s[:, 0:512], b2=ps[ub][:]:
                      e.tensor_mul(o, a2, b2), waits=((sSC, scv),) + hw)
                    bank_consumed(gb, sVE, sVE.n + 1)
                    bank_consumed(ub, sVE, sVE.n + 1)
                    if e_ > 0:
                        fsl = ffw[:, (e_ % 2) * 2048:(e_ % 2) * 2048 + 2048]
                        E("vector", lambda e, o=hslot[:, 0:256], a2=hslot[:, 0:256],
                          b2=fsl[:, c0:c0 + 256]:
                          e.tensor_mul(o, a2, b2), waits=((sVE, ffw_free[e_]),))
                        E("vector", lambda e, o=hslot[:, 256:512], a2=hslot[:, 256:512],
                          b2=fsl[:, c0:c0 + 256]:
                          e.tensor_mul(o, a2, b2), incs=((sVE, 1),))
                    else:
                        E("vector", lambda e, o=hslot[:, 0:1], a2=hslot[:, 0:1],
                          b2=hslot[:, 0:1]: e.tensor_copy(o, a2), incs=((sVE, 1),))
                    vfence()
                    hv = sVE.n
                    last_t1[0] = hv
                    for ct2 in range(2):
                        for ot in range(8):
                            st_flag = (hf == 0 and ct2 == 0)
                            sp_flag = (hf == 1 and ct2 == 1)
                            obk = ps[ot // 2][:, (ot % 2) * 256:(ot % 2) * 256 + 256]
                            E("tensor", lambda e, o=obk,
                              l=AW[:, base + 8192 + (2 * hf + ct2) * 1024 + ot * 128:
                                    base + 8192 + (2 * hf + ct2) * 1024 + ot * 128 + 128],
                              r2=hslot[:, ct2 * 256:(ct2 + 1) * 256],
                              st=(st_flag and ot % 2 == 0), sp=sp_flag:
                              e.matmul(o, l, r2, start=st, stop=sp, skip_group_check=True),
                              waits=((sVE, hv),) + (bank_wait(ot // 2) if (hf == 0 and ct2 == 0 and ot % 2 == 0) else ()),
                              incs=((sPE, 1),) if (ct2 == 1 and ot == 7) else ())
                    hp_free[(e_, ch, hf)] = sPE.n
                pv2 = sPE.n
                if e_ == 16 and ch >= 2:
                    emit_combine(ch - 2)
                slot = (e_ * 8 + ch) % 2
                stg = oev if slot == 0 else ofin
                sw = ((dW, stg_free.get(slot, 0)),) + comb_guard.get(slot, ())
                for bb in range(4):
                    E("vector", lambda e, o=stg[:, bb * 512:(bb + 1) * 512], s=ps[bb][:]:
                      e.tensor_copy(o, s),
                      waits=((sPE, pv2),) + sw, incs=(((sVE, 1),) if bb == 3 else ()))
                    bank_consumed(bb, sVE, sVE.n + (1 if bb < 3 else 0))
                vfence()
                ev = sVE.n
                aw2 = ((dW, acc_add.get((e_ - 1, ch), 0)),)
                E("gpsimd", lambda e,
                  d=ffacc.rearrange("(ot p) t2 -> p ot t2", p=128)[:, :, c0:c0 + 256],
                  s=stg[:, 0:2048].rearrange("p (ot t2) -> p ot t2", ot=8),
                  op=(ALU.add if e_ > 0 else ALU.bypass):
                  e.dma_start(out=d, in_=s, accum_op=op),
                  waits=((sVE, ev),) + aw2, incs=((dW, 16),))
                acc_add[(e_, ch)] = dW.n
                stg_free[slot] = dW.n
            wfree[e_] = sPE.n
        for ch in range(6, 8):
            emit_combine(ch)
        E("scalar", None, waits=((dX, dX.n),))

    P0()
    P0_wattn()
    P1()
    P1_dbg()
    if STAGE >= 2:
        P2()
        P0c()
        P2_dbg()
    if STAGE >= 3:
        P3()
        P3_dbg()
    if STAGE >= 4:
        P4()
        P4_dbg()
    if STAGE >= 5:
        P5()
        P5_dbg()
    if STAGE >= 6:
        P6()

    block = es.enter_context(nc.Block())
    with es:
        p.finalize(block)
    return nc


def make_inputs(inputs):
    x = np.asarray(inputs["L0_attn_x"], dtype=np.float32)
    masks_all = []
    in_maps = []
    shared = dict(
        W_attn=np.asarray(inputs["L0_attn_W_attn"], np.float32),
        b_attn=np.asarray(inputs["L0_attn_b_attn"], np.float32),
        W_proj=np.asarray(inputs["L0_attn_W_proj"], np.float32),
        b_proj=np.asarray(inputs["L0_attn_b_proj"], np.float32),
        gate_w=np.asarray(inputs["L0_ffn_gate_w"], np.float32),
        gate_b=np.asarray(inputs["L0_ffn_gate_b"], np.float32),
        sh_wg=np.asarray(inputs["L0_ffn_shared_wg"], np.float32),
        sh_wu=np.asarray(inputs["L0_ffn_shared_wu"], np.float32),
        sh_wd=np.asarray(inputs["L0_ffn_shared_wd"], np.float32),
        ex_wg=np.asarray(inputs["L0_ffn_experts_wg"], np.float32),
        ex_wu=np.asarray(inputs["L0_ffn_experts_wu"], np.float32),
        ex_wd=np.asarray(inputs["L0_ffn_experts_wd"], np.float32),
        identf=np.eye(128, dtype=np.float32),
    )
    for c in range(8):
        b, m = divmod(c, 4)
        o = 384 - 128 * m
        x_own = np.concatenate([x[b, 512 * t + o: 512 * t + o + 128] for t in range(NT)])
        kk = np.arange(128)[:, None]
        qq = np.arange(128)[None, :]
        mk = np.stack([( (128 * r + kk) <= (o + qq)).astype(BF) for r in range(4)])
        im = dict(shared)
        im["x_own"] = np.ascontiguousarray(x_own)
        im["selb"] = np.kron(np.eye(16, dtype=np.float32), np.ones((1, 128), np.float32)).astype(BF)
        im["x_kv"] = np.ascontiguousarray(x[b])
        im["masks"] = mk
        in_maps.append(im)
    return in_maps


def kernel(**inputs):
    in_maps = make_inputs(inputs)
    nc = build_nc()
    res = run_bass_kernel_spmd(nc, in_maps, core_ids=list(range(8)))
    x = np.asarray(inputs["L0_attn_x"], np.float32)
    out = np.empty_like(x)
    for c in range(8):
        b, m = divmod(c, 4)
        o = 384 - 128 * m
        oc = res.results[c]["out_c"]
        for t in range(NT):
            out[b, 512 * t + o: 512 * t + o + 128] = oc[128 * t: 128 * (t + 1)]
    return out


def debug_run(inputs):
    in_maps = make_inputs(inputs)
    nc = build_nc()
    res = run_bass_kernel_spmd(nc, in_maps, core_ids=list(range(8)))
    return res, in_maps
